# version 14
# baseline (speedup 1.0000x reference)
import contextlib
import numpy as np

import concourse.bass as bass
import concourse.mybir as mybir
from concourse.bass_utils import run_bass_kernel_spmd

F32 = mybir.dt.float32
BF16 = mybir.dt.bfloat16
I32 = mybir.dt.int32
AF = mybir.ActivationFunctionType
ALU = mybir.AluOpType
AX = mybir.AxisListType

ENGINES = ("tensor", "vector", "scalar", "gpsimd", "sync")

D = 1024
DI = 2048
DH = 1024
NCH = 8
S = 4096
SH = 2048
STR = 256
NSTR = 16
NW = 8
FW = 4
KS = 16
RNK = 64
H = 4096
NHC = 32
NDC = 8
PAIRS = [[0, 1], [2, 3], [4, 5], [6, 7]]
ALL8 = [list(range(8))]
RND = 256.5


class Buf:
    __slots__ = ("name", "w", "r")

    def __init__(self, name):
        self.name = name
        self.w = None
        self.r = []


class _Op:
    __slots__ = ("eng", "fn", "waits", "incs", "tag")

    def __init__(self, eng, fn, waits, incs, tag):
        self.eng, self.fn, self.waits, self.incs, self.tag = eng, fn, waits, incs, tag


class Fw:

    def __init__(self, nc, dma_slots=8):
        self.nc = nc
        self.ops = {e: [] for e in ENGINES}
        self.sem_count = {}
        self.hw = {}
        self.sem_handles = {}
        self.sem_names = []
        self.dma_slots = dma_slots
        self._dma_rr = {}

    def _sem(self, name):
        if name not in self.sem_count:
            self.sem_count[name] = 0
            self.sem_names.append(name)
        return name

    def buf(self, name):
        return Buf(name)

    def _collect_waits(self, eng, reads, writes):
        need = {}

        def add(tok):
            if tok is None:
                return
            s, v = tok
            if need.get(s, -1) < v:
                need[s] = v

        for b in reads:
            add(b.w)
        for b in writes:
            add(b.w)
            for t in b.r:
                add(t)
        waits = []
        for s, v in need.items():
            k = (eng, s)
            if self.hw.get(k, 0) >= v:
                continue
            self.hw[k] = v
            waits.append((s, v))
        return waits

    def op(self, eng, fn, reads=(), writes=(), tag=""):
        waits = self._collect_waits(eng, reads, writes)
        sem = self._sem("prog_" + eng)
        self.sem_count[sem] += 1
        tok = (sem, self.sem_count[sem])
        for b in writes:
            b.w = tok
            b.r = []
        for b in reads:
            b.r.append(tok)
        self.ops[eng].append(_Op(eng, fn, waits, [(sem, 1)], tag))
        return tok

    def dma(self, eng, fn, reads=(), writes=(), tag=""):
        rr = self._dma_rr.get(eng, 0)
        self._dma_rr[eng] = rr + 1
        sem = self._sem(f"dma_{eng}{rr % self.dma_slots}")
        waits = self._collect_waits(eng, reads, writes)
        prev = self.sem_count[sem]
        if prev:
            k = (eng, sem)
            if self.hw.get(k, 0) < prev:
                self.hw[k] = prev
                waits = waits + [(sem, prev)]
        self.sem_count[sem] += 16
        tok = (sem, self.sem_count[sem])
        for b in writes:
            b.w = tok
            b.r = []
        for b in reads:
            b.r.append(tok)
        self.ops[eng].append(_Op(eng, fn, waits, [(sem, 16)], tag))
        return tok

    def collective(self, fn, reads=(), writes=(), tag="cc"):
        waits = self._collect_waits("gpsimd", reads, writes)
        sem = self._sem("cc")
        self.sem_count[sem] += 1
        tok = (sem, self.sem_count[sem])
        for b in writes:
            b.w = tok
            b.r = []
        for b in reads:
            b.r.append(tok)
        self.ops["gpsimd"].append(_Op("gpsimd", fn, waits, [(sem, 1)], tag))
        return tok

    def phase_barrier(self):
        snapshot = [(s, v) for s, v in self.sem_count.items() if v > 0]
        for e in ENGINES:
            waits = []
            for s, v in snapshot:
                k = (e, s)
                if self.hw.get(k, 0) >= v:
                    continue
                self.hw[k] = v
                waits.append((s, v))
            if waits:
                self.ops[e].append(_Op(e, None, waits, [], "phase_barrier"))

    def barrier_wait(self, eng, tok, tag="barrier"):
        if tok is None:
            return
        s, v = tok
        k = (eng, s)
        if self.hw.get(k, 0) >= v:
            return
        self.hw[k] = v
        self.ops[eng].append(_Op(eng, None, [(s, v)], [], tag))

    def final_dma_barrier(self, eng):
        for s, v in self.sem_count.items():
            if not (s.startswith("dma_") or s == "cc"):
                continue
            k = (eng, s)
            if self.hw.get(k, 0) >= v:
                continue
            self.hw[k] = v
            self.ops[eng].append(_Op(eng, None, [(s, v)], [], "final"))

    def finalize(self, exit_ctx):
        nc = self.nc
        for name in self.sem_names:
            self.sem_handles[name] = exit_ctx.enter_context(nc.semaphore(name))
        block = exit_ctx.enter_context(nc.Block())
        for eng_name in ENGINES:
            ops = self.ops[eng_name]
            if not ops:
                continue

            def make(eng_ops):
                def emit(eng):
                    for o in eng_ops:
                        for s, v in o.waits:
                            eng.wait_ge(self.sem_handles[s], v)
                        if o.fn is None:
                            continue
                        inst = o.fn(eng)
                        assert inst is not None, f"op {o.tag} returned None"
                        for s, n in o.incs:
                            inst.then_inc(self.sem_handles[s], n)
                return emit

            getattr(block, eng_name)(make(ops))


def build_nc():
    nc = bass.Bass("TRN2", target_bir_lowering=False)
    fw = Fw(nc)

    def IN(name, shape, dtype=F32):
        return nc.dram_tensor(name, shape, dtype, kind="ExternalInput")

    x_own = IN("x_own", [SH, D])
    c_own = IN("c_own", [1, D])
    wmod = IN("wmod", [D, 6 * D])
    w_in_own = IN("w_in_own", [D, 2 * DH])
    conv_w_c = IN("conv_w_c", [128, NCH, 4])
    conv_b_c = IN("conv_b_c", [128, NCH])
    w_x_own = IN("w_x_own", [DH, 96])
    w_dt_own = IN("w_dt_own", [RNK, DH])
    b_dt_c = IN("b_dt_c", [128, NCH])
    a_log_c = IN("a_log_c", [128, NCH, KS])
    d_c = IN("d_c", [128, NCH])
    w_out_own = IN("w_out_own", [DH, D])
    w1t = IN("w1t", [D, H])
    b1_c = IN("b1_c", [128, NHC])
    w2t = IN("w2t", [H, D])
    b2_c = IN("b2_c", [128, NDC])
    ident_in = IN("ident_in", [128, 128])
    ones_r = IN("ones_r", [1, 128])
    tri_in = IN("tri_in", [128, 128])
    const_in = IN("const_in", [128, 4])

    out_own = nc.dram_tensor("out_own", [SH, D], F32, kind="ExternalOutput")

    hn_in = nc.dram_tensor("hn_in", [D, SH], BF16)
    hn_out = nc.dram_tensor("hn_out", [2, D, SH], BF16)
    ssmp_in = [nc.dram_tensor(f"ssmp_in{i}", [96, STR], F32) for i in range(NSTR)]
    ssmp_out = [nc.dram_tensor(f"ssmp_out{i}", [96, STR], F32) for i in range(NSTR)]
    att_in = [nc.dram_tensor(f"att_in{i}", [2, STR, D], BF16) for i in range(NW)]
    att_out = [nc.dram_tensor(f"att_out{i}", [STR, D], BF16) for i in range(NW)]
    mmx_in = nc.dram_tensor("mmx_in", [1, 2], F32)
    mmx_out = nc.dram_tensor("mmx_out", [1, 2], F32)
    gmx_in = nc.dram_tensor("gmx_in", [1, 2], F32)
    gmx_out = nc.dram_tensor("gmx_out", [1, 2], F32)
    xres_b = nc.dram_tensor("xres_b", [SH, D], F32)
    g_b = nc.dram_tensor("g_b", [H, SH], F32)
    wq1_b = nc.dram_tensor("wq1_b", [D, H], BF16)

    ex = contextlib.ExitStack()

    def sb(name, shape, dtype=F32, stack=None):
        return (stack or ex).enter_context(nc.sbuf_tensor(name, shape, dtype))

    def psum(name, shape, dtype=F32):
        return ex.enter_context(nc.psum_tensor(name, shape, dtype))

    pHh = [psum("pHa", [128, 512]), psum("pHb", [128, 512])]
    yTh = [psum("yTa", [128, 512]), psum("yTb", [128, 512])]
    p0 = psum("p0", [128, 512])
    p1 = psum("p1", [128, 512])
    p6b = psum("p6b", [128, 1024], BF16)
    p7 = psum("p7", [128, 512])

    bufs = {}

    def B(name):
        if name not in bufs:
            bufs[name] = fw.buf(name)
        return bufs[name]

    def vop(fn, reads, writes, tag=""):
        return fw.op("vector", fn, reads, writes, tag)

    def sop(fn, reads, writes, tag=""):
        return fw.op("scalar", fn, reads, writes, tag)

    def top(fn, reads, writes, tag=""):
        return fw.op("tensor", fn, reads, writes, tag)

    def dma(fn, reads, writes, tag="", eng="sync"):
        return fw.dma(eng, fn, reads, writes, tag)

    identf = sb("identf", [128, 128])
    identb = sb("identb", [128, 128], BF16)
    onesrf = sb("onesrf", [1, 128])
    onesrb = sb("onesrb", [1, 128], BF16)
    onescb = sb("onescb", [128, 1], BF16)
    trif = sb("trif", [128, 128])
    trib = sb("trib", [128, 128], BF16)
    akb = sb("akb", [128, KS])
    nakb = sb("nakb", [128, KS])
    constc = sb("constc", [128, 4])

    w_in_s = sb("w_in_s", [128, NCH, 2 * DH], BF16)
    w_out_s = sb("w_out_s", [128, NCH, D], BF16)
    w_x_s = sb("w_x_s", [128, NCH, 96], BF16)
    w_dt_s = sb("w_dt_s", [RNK, DH], BF16)
    a_cols = sb("a_cols", [128, NCH, KS])
    d_cols = sb("d_cols", [128, NCH])
    cw_cols = sb("cw_cols", [128, NCH, 4])
    cb_cols = sb("cb_cols", [128, NCH])
    bdt_cols = sb("bdt_cols", [128, NCH])
    sa_cols = sb("sa_cols", [128, 16])
    mod48 = sb("mod48", [48, 128])
    modT = sb("modT", [128, 48])
    gacol = sb("gacol", [128, NCH])
    gbcol = sb("gbcol", [128, NDC])
    scal = sb("scal", [1, 32])
    scalb = sb("scalb", [128, 16])

    dma(lambda e: e.dma_start(out=identf[:, :], in_=ident_in.ap()), [], [B("identf")])
    dma(lambda e: e.dma_start(out=constc[:, :], in_=const_in.ap()), [], [B("constc")])
    dma(lambda e: e.dma_start(out=trif[:, :], in_=tri_in.ap()), [], [B("trif")])
    vop(lambda v: v.tensor_copy(trib[:, :], trif[:, :]), [B("trif")], [B("trib")])
    nc.const_aps.register(F32, 1e-5, constc[:, 0:1])
    nc.const_aps.register(F32, 1.0, constc[:, 1:2])
    nc.const_aps.register(F32, RND, constc[:, 2:3])
    dma(lambda e: e.dma_start(out=onesrf[:, :], in_=ones_r.ap()), [], [B("onesrf")])
    vop(lambda v: v.tensor_copy(identb[:, :], identf[:, :]), [B("identf")], [B("identb")])
    vop(lambda v: v.tensor_copy(onesrb[:, :], onesrf[:, :]), [B("onesrf")], [B("onesrb")])
    top(lambda t: t.transpose(p6b[:, 0:1], onesrb[:, :], identb[0:1, 0:1]),
        [B("onesrb"), B("identb")], [B("p6b")])
    sop(lambda s: s.copy(out=onescb[:, :], in_=p6b[:, 0:1]), [B("p6b")], [B("onescb")])

    dma(lambda e: e.dma_start(out=cw_cols.ap(), in_=conv_w_c.ap()), [], [B("cw")])
    dma(lambda e: e.dma_start(out=cb_cols.ap(), in_=conv_b_c.ap()), [], [B("cb")])
    dma(lambda e: e.dma_start(out=bdt_cols.ap(), in_=b_dt_c.ap()), [], [B("bdt")])
    dma(lambda e: e.dma_start(out=d_cols.ap(), in_=d_c.ap()), [], [B("dcol")])
    dma(lambda e: e.dma_start(out=a_cols.ap(), in_=a_log_c.ap()), [], [B("acol")])
    sop(lambda s: s.activation(a_cols[:, :, :], a_cols[:, :, :], AF.Exp),
        [B("acol")], [B("acol")])
    vop(lambda v: v.tensor_scalar(a_cols[:, :, :], a_cols[:, :, :], -1.0, None, op0=ALU.mult),
        [B("acol")], [B("acol")])
    for k in range(KS):
        top(lambda t, k=k: t.matmul(p7[:, 0:1], onesrf[:, :], a_cols[0:1, 0, k:k + 1],
                                    start=True, stop=True),
            [B("acol"), B("onesrf"), B("p7")], [B("p7")], "akbc")
        sop(lambda s, k=k: s.copy(out=akb[:, k:k + 1], in_=p7[:, 0:1]),
            [B("p7")], [B("akb")])
    vop(lambda v: v.tensor_scalar(nakb[:, :], akb[:, :], -1.0, None, op0=ALU.mult),
        [B("akb")], [B("nakb")])

    with contextlib.ExitStack() as wx_ex:
        wx_f = sb("wx_f", [128, NCH, 96], F32, wx_ex)
        wdt_f = sb("wdt_f", [RNK, DH], F32, wx_ex)
        dma(lambda e: e.dma_start(out=wx_f.ap(),
                                  in_=w_x_own.ap().rearrange("(c p) n -> p c n", p=128)),
            [], [B("wxf")])
        vop(lambda v: v.tensor_copy(w_x_s[:, :, :], wx_f[:, :, :]), [B("wxf")], [B("wxs")])
        dma(lambda e: e.dma_start(out=wdt_f.ap(), in_=w_dt_own.ap()), [], [B("wdtf")])
        vop(lambda v: v.tensor_copy(w_dt_s[:, :], wdt_f[:, :]), [B("wdtf")], [B("wdts")])

        cc = sb("cc", [1, D], F32, wx_ex)
        ccT = sb("ccT", [128, NDC], F32, wx_ex)
        ccTb = sb("ccTb", [128, NDC], BF16, wx_ex)
        wm_f2 = [sb(f"wm_f{i}", [128, 512], F32, wx_ex) for i in range(2)]
        wm_b2 = [sb(f"wm_b{i}", [128, 512], BF16, wx_ex) for i in range(2)]
        mrow = sb("mrow", [1, 512], F32, wx_ex)
        dma(lambda e: e.dma_start(out=cc.ap(), in_=c_own.ap()), [], [B("cc")])
        sop(lambda s: s.activation(cc[:, :], cc[:, :], AF.Silu), [B("cc")], [B("cc")])
        for dc in range(NDC):
            top(lambda t, dc=dc: t.transpose(p7[:, 0:1], cc[0:1, bass.ts(dc, 128)], identf[0:1, 0:1]),
                [B("cc"), B("identf")], [B("p7")], "ccT")
            sop(lambda s, dc=dc: s.copy(out=ccT[:, dc:dc + 1], in_=p7[:, 0:1]),
                [B("p7")], [B("ccT")])
        vop(lambda v: v.tensor_copy(ccTb[:, :], ccT[:, :]), [B("ccT")], [B("ccTb")])
        for g in range(12):
            for dc in range(NDC):
                bi = dc % 2
                wm_f, wm_b = wm_f2[bi], wm_b2[bi]
                dma(lambda e, g=g, dc=dc, wm_f=wm_f: e.dma_start(
                        out=wm_f[:, :], in_=wmod.ap()[bass.ts(dc, 128), bass.ts(g, 512)]),
                    [B(f"wmf{bi}")], [B(f"wmf{bi}")], "wmod_ld")
                vop(lambda v, wm_f=wm_f, wm_b=wm_b: v.tensor_copy(wm_b[:, :], wm_f[:, :]),
                    [B(f"wmf{bi}")], [B(f"wmb{bi}")])
                top(lambda t, dc=dc, wm_b=wm_b: t.matmul(p7[0:1, :], ccTb[:, dc:dc + 1], wm_b[:, :],
                                              start=(dc == 0), stop=(dc == NDC - 1)),
                    [B("ccTb"), B(f"wmb{bi}"), B("p7")], [B("p7")], "modmm")
            sop(lambda s: s.copy(out=mrow[:, :], in_=p7[0:1, :]), [B("p7")], [B("mrow")])
            dma(lambda e, g=g: e.dma_start(
                    out=mod48.ap()[g * 4:(g + 1) * 4, :], in_=mrow.ap()),
                [B("mrow")], [B("mod48")], "mod48")
        top(lambda t: t.transpose(p7[:, 0:48], mod48[:, :], identf[0:48, 0:48]),
            [B("mod48"), B("identf")], [B("p7")])
        sop(lambda s: s.copy(out=modT[:, :], in_=p7[:, 0:48]), [B("p7")], [B("modT")])
        sgn = sb("sgn", [128, NCH], F32, wx_ex)
        mag = sb("mag", [128, NCH], F32, wx_ex)
        for (dst, base) in ((gacol, 16), (gbcol, 40)):
            vop(lambda v, base=base: v.tensor_scalar(sgn[:, :], modT[:, base:base + 8], 0.0, None, op0=ALU.is_ge),
                [B("modT")], [B("sgn")])
            vop(lambda v: v.tensor_scalar(sgn[:, :], sgn[:, :], 2.0, -1.0, op0=ALU.mult, op1=ALU.add),
                [B("sgn")], [B("sgn")])
            sop(lambda s, base=base: s.activation(mag[:, :], modT[:, base:base + 8], AF.Abs),
                [B("modT")], [B("mag")])
            vop(lambda v: v.tensor_scalar(mag[:, :], mag[:, :], 0.25, None, op0=ALU.max),
                [B("mag")], [B("mag")])
            vop(lambda v, dst=dst: v.tensor_tensor(dst[:, :], sgn[:, :], mag[:, :], ALU.mult),
                [B("sgn"), B("mag")], [B(dst.name)])

        capac = sb("capac", [128, NCH], F32, wx_ex)
        saTb = sb("saTb", [128, NCH], BF16, wx_ex)
        vop(lambda v: v.tensor_scalar(capac[:, :], modT[:, 8:16], 1.0, None, op0=ALU.add),
            [B("modT")], [B("capac")])
        vop(lambda v: v.tensor_copy(saTb[:, :], modT[:, 0:8]), [B("modT")], [B("saTb")])
        wi_f2 = [sb(f"wi_f{i}", [128, 512], F32, wx_ex) for i in range(2)]
        wi_b2 = [sb(f"wi_b{i}", [128, 512], BF16, wx_ex) for i in range(2)]
        sarow = sb("sarow", [1, 512], F32, wx_ex)
        for g in range(4):
            for dc in range(NDC):
                bi = dc % 2
                wi_f, wi_b = wi_f2[bi], wi_b2[bi]
                dma(lambda e, g=g, dc=dc, wi_f=wi_f: e.dma_start(
                        out=wi_f[:, :], in_=w_in_own.ap()[bass.ts(dc, 128), bass.ts(g, 512)]),
                    [B(f"wif{bi}")], [B(f"wif{bi}")], "win_ld")
                vop(lambda v, wi_f=wi_f, wi_b=wi_b: v.tensor_copy(wi_b[:, :], wi_f[:, :]),
                    [B(f"wif{bi}")], [B(f"wib{bi}")])
                top(lambda t, dc=dc, wi_b=wi_b: t.matmul(p7[0:1, :], saTb[:, dc:dc + 1], wi_b[:, :],
                                              start=(dc == 0), stop=(dc == NDC - 1)),
                    [B("saTb"), B(f"wib{bi}"), B("p7")], [B("p7")], "sarow")
                vop(lambda v, g=g, dc=dc, wi_f=wi_f: v.tensor_scalar(
                        w_in_s[:, dc, bass.ts(g, 512)], wi_f[:, :], capac[:, dc:dc + 1], None, op0=ALU.mult),
                    [B(f"wif{bi}"), B("capac")], [B("wins")], "winfold")
            sop(lambda s: s.copy(out=sarow[:, :], in_=p7[0:1, :]), [B("p7")], [B("sarow")])
            for q in range(4):
                top(lambda t, q=q: t.transpose(p7[:, 0:1], sarow[0:1, bass.ts(q, 128)], identf[0:1, 0:1]),
                    [B("sarow"), B("identf"), B("p7")], [B("p7")], "saT")
                sop(lambda s, g=g, q=q: s.copy(out=sa_cols[:, 4 * g + q:4 * g + q + 1], in_=p7[:, 0:1]),
                    [B("p7")], [B("sacols")])

        wo_f2 = [sb(f"wo_f{i}", [128, 512], F32, wx_ex) for i in range(2)]
        for c in range(NCH):
            for g in range(2):
                bi = (c * 2 + g) % 2
                wo_f = wo_f2[bi]
                dma(lambda e, c=c, g=g, wo_f=wo_f: e.dma_start(
                        out=wo_f[:, :], in_=w_out_own.ap()[bass.ts(c, 128), bass.ts(g, 512)]),
                    [B(f"wof{bi}")], [B(f"wof{bi}")], "wout_ld")
                vop(lambda v, c=c, g=g, wo_f=wo_f: v.tensor_copy(w_out_s[:, c, bass.ts(g, 512)], wo_f[:, :]),
                    [B(f"wof{bi}")], [B("wouts")])

    mam = contextlib.ExitStack()
    xtile2 = [sb(f"xtile{i}", [128, D], F32, mam) for i in range(2)]
    sqt = sb("sqt", [128, D], F32, mam)
    stats2 = [sb(f"stats{i}", [128, 8], F32, mam) for i in range(2)]
    hn_tile2 = [sb(f"hn_tile{i}", [128, D], BF16, mam) for i in range(2)]
    hnT_t = sb("hnT_t", [128, NDC, 128], BF16, mam)

    for t_i in range(SH // 128):
        bi = t_i % 2
        xtile, stats, hn_tile = xtile2[bi], stats2[bi], hn_tile2[bi]
        xb_, stb, hnb_ = B(f"xtile{bi}"), B(f"stats{bi}"), B(f"hn_tile{bi}")
        dma(lambda e, t_i=t_i, xtile=xtile: e.dma_start(out=xtile[:, :], in_=x_own.ap()[bass.ts(t_i, 128), :]),
            [xb_], [xb_], "lna_x")
        vop(lambda v, xtile=xtile, stats=stats: v.tensor_reduce(stats[:, 0:1], xtile[:, :], AX.X, ALU.add),
            [xb_], [stb])
        sop(lambda s, xtile=xtile, stats=stats: s.activation(sqt[:, :], xtile[:, :], AF.Square,
                                                             accum_out=stats[:, 1:2]),
            [xb_, B("sqt")], [B("sqt"), stb])
        vop(lambda v, stats=stats: v.tensor_scalar(stats[:, 2:3], stats[:, 0:1], 1.0 / D, None, op0=ALU.mult),
            [stb], [stb])
        sop(lambda s, stats=stats: s.activation(stats[:, 3:4], stats[:, 2:3], AF.Square),
            [stb], [stb])
        vop(lambda v, stats=stats: v.tensor_scalar(stats[:, 4:5], stats[:, 1:2], 1.0 / D, stats[:, 3:4],
                                      op0=ALU.mult, op1=ALU.subtract),
            [stb], [stb])
        sop(lambda s, stats=stats: s.activation(stats[:, 5:6], stats[:, 4:5], AF.Sqrt, bias=1e-5),
            [stb], [stb])
        vop(lambda v, stats=stats: v.reciprocal(stats[:, 6:7], stats[:, 5:6]), [stb], [stb])
        vop(lambda v, xtile=xtile, stats=stats, hn_tile=hn_tile: v.tensor_scalar(
                hn_tile[:, :], xtile[:, :], stats[:, 2:3], stats[:, 6:7],
                op0=ALU.subtract, op1=ALU.mult),
            [xb_, stb], [hnb_])
        for dc in range(NDC):
            top(lambda t, dc=dc, hn_tile=hn_tile: t.transpose(p6b[:, bass.ts(dc, 128)],
                                                              hn_tile[:, bass.ts(dc, 128)], identb[:, :]),
                [hnb_, B("identb"), B("p6b")], [B("p6b")], "lnaT")
        sop(lambda s: s.copy(out=hnT_t[:, :, :], in_=p6b.ap().rearrange("p (c n) -> p c n", c=NDC)),
            [B("p6b")], [B("hnT")], "lnaTe")
        dma(lambda e, t_i=t_i: e.dma_start(
                out=hn_in.ap().rearrange("(c p) s -> p c s", p=128)[:, :, bass.ts(t_i, 128)],
                in_=hnT_t.ap()),
            [B("hnT")], [B("hn_in_d")], "lna_st")

    tok_hn = fw.collective(
        lambda g: g.collective_compute("AllGather", ALU.bypass, replica_groups=PAIRS,
                                       ins=[hn_in.ap().opt()], outs=[hn_out.ap().opt()]),
        reads=[B("hn_in_d")], writes=[B("hn_out_d")], tag="hn_ag")

    hn_str2 = [sb(f"hn_str{i}", [128, NDC, STR], BF16, mam) for i in range(2)]
    xin_str = sb("xin_str", [128, NCH, STR + 3], F32, mam)
    dt_str = sb("dt_str", [128, NCH, STR], F32, mam)
    xact2 = [sb(f"xact_str{i}", [128, NCH, STR], F32, mam) for i in range(2)]
    xact_b = sb("xact_b", [128, NCH, STR], BF16, mam)
    sz_str2 = [sb(f"sz_str{i}", [128, NCH, STR], BF16, mam) for i in range(2)]
    q_str = sb("q_str", [128, NCH, STR], BF16, mam)
    ssmp_t = sb("ssmp_t", [96, STR], F32, mam)
    dtr_b = sb("dtr_b", [RNK, STR], BF16, mam)
    cv2b = [sb(f"cv{i}", [128, STR], F32, mam) for i in range(2)]
    wv = [sb(f"wv{i}", [128, 1024], BF16, mam) for i in range(4)]
    wt = [sb(f"wt{i}", [128, 1024], BF16, mam) for i in range(4)]
    crow = sb("crow", [KS, 1024], BF16, mam)
    egp = [sb(f"eg{i}", [128, 1024], F32, mam) for i in range(4)]
    dtTM = sb("dtTM", [128, 1024], F32, mam)
    ySB = dtTM
    uTM = [sb(f"uTM{i}", [128, 1024], F32, mam) for i in range(2)]
    phs = [sb(f"phs{i}", [128, 1024], F32, mam) for i in range(2)]
    BCt2 = [sb(f"BCt{i}", [128, 2, 32], F32, mam) for i in range(2)]
    diag = [sb(f"diag{i}", [128, 128], BF16, mam) for i in range(2)]
    qt_t2 = [sb(f"qt_t{i}", [128, STR], F32, mam) for i in range(2)]
    xdb2 = [sb(f"xdb{i}", [128, NCH, STR], BF16, mam) for i in range(2)]
    attf2 = [sb(f"attf{i}", [128, STR], BF16, mam) for i in range(2)]
    attnTM = sb("attnTM", [128, 2, D], BF16, mam)
    rs_t = sb("rs_t", [128, 1, D], BF16, mam)
    xown_t = None

    vop(lambda v: v.memset(xin_str[:, :, 0:3], 0.0), [], [B(f"xin{c}") for c in range(NCH)])

    def gen_A1(st):
        if st >= NSTR:
            return []
        o = []
        p = st % 2
        hn_str = hn_str2[p]
        sz_str = sz_str2[p]
        hnb = B(f"hn_str{p}")
        szb = B(f"sz_str{p}")
        xact_str = xact2[p]
        xdb = xdb2[p]
        blk = st // 8
        scol = (st % 8) * STR
        o.append(lambda: dma(lambda e, blk=blk, scol=scol, hn_str=hn_str: e.dma_start(
                out=hn_str.ap(),
                in_=hn_out.ap().rearrange("b (c p) s -> p b c s", p=128)[:, blk, :, scol:scol + STR]),
            [hnb, B("hn_out_d")], [hnb], "hn_ld"))
        for c in range(NCH):
            pb, pbuf = (p0, B("p0")) if c % 2 == 0 else (p1, B("p1"))
            o.append(lambda c=c, pb=pb, pbuf=pbuf, hn_str=hn_str, hnb=hnb: top(
                lambda t, c=c, pb=pb, hn_str=hn_str: [
                    t.matmul(pb[:, 0:STR], w_in_s[:, dc, bass.ts(c, 128)], hn_str[:, dc, :],
                             start=(dc == 0), stop=(dc == NDC - 1))
                    for dc in range(NDC)][-1],
                [B("wins"), hnb, pbuf], [pbuf], "win_mm_x"))
            o.append(lambda c=c, pb=pb, pbuf=pbuf: vop(
                lambda v, c=c, pb=pb: v.tensor_scalar(xin_str[:, c, 3:STR + 3], pb[:, 0:STR],
                                                      sa_cols[:, c:c + 1], None, op0=ALU.add),
                [pbuf, B("sacols")], [B(f"xin{c}")], "xin_ev"))
            o.append(lambda c=c, p=p, xdb=xdb: vop(
                lambda v, c=c, xdb=xdb: v.tensor_scalar(xdb[:, c, :], xin_str[:, c, 3:STR + 3],
                                                        d_cols[:, c:c + 1], None, op0=ALU.mult),
                [B(f"xin{c}"), B("dcol")], [B(f"xdb{c}_{p}")], "xd"))
        for c in range(NCH):
            pb, pbuf = (p0, B("p0")) if c % 2 == 0 else (p1, B("p1"))
            o.append(lambda c=c, pb=pb, pbuf=pbuf, hn_str=hn_str, hnb=hnb: top(
                lambda t, c=c, pb=pb, hn_str=hn_str: [
                    t.matmul(pb[:, 0:STR], w_in_s[:, dc, DH + 128 * c:DH + 128 * (c + 1)], hn_str[:, dc, :],
                             start=(dc == 0), stop=(dc == NDC - 1))
                    for dc in range(NDC)][-1],
                [B("wins"), hnb, pbuf], [pbuf], "win_mm_z"))
            o.append(lambda c=c, pb=pb, pbuf=pbuf, sz_str=sz_str, szb=szb: sop(
                lambda s, c=c, pb=pb, sz_str=sz_str: s.activation(sz_str[:, c, :], pb[:, 0:STR],
                                                                  AF.Silu, bias=sa_cols[:, 8 + c:9 + c]),
                [pbuf, B("sacols")], [szb], "sz_ev"))
        for c in range(NCH):
            cv = cv2b[c % 2]
            cvb = B(f"cv{c % 2}")
            o.append(lambda c=c, cv=cv, cvb=cvb: vop(
                lambda v, c=c, cv=cv: v.tensor_scalar(cv[:, :], xin_str[:, c, 0:STR], cw_cols[:, c, 0:1],
                                                      None, op0=ALU.mult),
                [B(f"xin{c}"), B("cw"), cvb], [cvb], "conv0"))
            for j in range(1, 4):
                o.append(lambda c=c, j=j, cv=cv, cvb=cvb: vop(
                    lambda v, c=c, j=j, cv=cv: v.scalar_tensor_tensor(
                        cv[:, :], xin_str[:, c, j:STR + j], cw_cols[:, c, j:j + 1], cv[:, :],
                        ALU.mult, ALU.add),
                    [B(f"xin{c}"), B("cw"), cvb], [cvb], "conv"))
            o.append(lambda c=c, p=p, cv=cv, cvb=cvb, xact_str=xact_str: sop(
                lambda s, c=c, cv=cv, xact_str=xact_str: s.activation(
                    xact_str[:, c, :], cv[:, :], AF.Silu, bias=cb_cols[:, c:c + 1]),
                [cvb, B("cb")], [B(f"xact{c}_{p}")], "xact"))
            o.append(lambda c=c, p=p, xact_str=xact_str: vop(
                lambda v, c=c, xact_str=xact_str: v.tensor_copy(xact_b[:, c, :], xact_str[:, c, :]),
                [B(f"xact{c}_{p}")], [B(f"xactb{c}")], "xact_b"))
        o.append(lambda: top(lambda t: [t.matmul(p0[0:96, 0:STR], w_x_s[:, c, :], xact_b[:, c, :],
                                start=(c == 0), stop=(c == NCH - 1)) for c in range(NCH)][-1],
            [B("wxs"), B("p0")] + [B(f"xactb{c}") for c in range(NCH)], [B("p0")], "ssmp_mm"))
        o.append(lambda: sop(lambda s: s.copy(out=ssmp_t[:, :], in_=p0[0:96, 0:STR]),
                             [B("p0")], [B("ssmp_t")]))
        o.append(lambda: dma(lambda e, st=st: e.dma_start(out=ssmp_in[st].ap(), in_=ssmp_t[:, :]),
            [B("ssmp_t")], [B(f"ssmp_in{st}")], "ssmp_st"))
        o.append(lambda: vop(lambda v: v.tensor_copy(xin_str[:, :, 0:3], xin_str[:, :, STR:STR + 3]),
            [B(f"xin{c}") for c in range(NCH)], [B(f"xin{c}") for c in range(NCH)], "halo"))
        return o

    def gen_A2(st):
        if st >= NSTR:
            return None, []
        p = st % 2
        xact_str = xact2[p]
        ar = lambda: fw.collective(
            lambda g, st=st: g.collective_compute("AllReduce", ALU.add, replica_groups=PAIRS,
                                                  ins=[ssmp_in[st].ap().opt()],
                                                  outs=[ssmp_out[st].ap().opt()]),
            reads=[B(f"ssmp_in{st}")], writes=[B(f"ssmp_out{st}")], tag="ssmp_ar")
        o = []
        o.append(lambda: dma(lambda e, st=st: e.dma_start(out=ssmp_t[:, :], in_=ssmp_out[st].ap()),
            [B("ssmp_t"), B(f"ssmp_out{st}")], [B("ssmp_t")], "ssmp_ld"))
        o.append(lambda: vop(lambda v: v.tensor_copy(dtr_b[:, :], ssmp_t[0:RNK, :]),
                             [B("ssmp_t")], [B("dtr_b")]))
        for tc in range(2):
            o.append(lambda tc=tc: top(
                lambda t, tc=tc: t.transpose(p7[:, 0:32], ssmp_t[64:96, bass.ts(tc, 128)],
                                             identf[64:96, 64:96]),
                [B("ssmp_t"), B("identf"), B("p7"), B("p7")], [B("p7")], "bcT"))
            o.append(lambda tc=tc, p=p: sop(
                lambda s, tc=tc: s.copy(out=BCt2[p][:, tc, :], in_=p7[:, 0:32]),
                [B("p7")], [B(f"BCt{p}")], "bcTe"))
        for c in range(NCH):
            pb, pbuf = (p0, B("p0")) if c % 2 == 0 else (p1, B("p1"))
            o.append(lambda c=c, pb=pb, pbuf=pbuf: top(
                lambda t, c=c, pb=pb: t.matmul(pb[:, 0:STR], w_dt_s[:, bass.ts(c, 128)], dtr_b[:, :],
                                               start=True, stop=True),
                [B("wdts"), B("dtr_b"), pbuf], [pbuf], "dt_mm"))
            o.append(lambda c=c, pb=pb, pbuf=pbuf: sop(
                lambda s, c=c, pb=pb: s.activation(dt_str[:, c, :], pb[:, 0:STR], AF.Exp,
                                                   bias=bdt_cols[:, c:c + 1]),
                [pbuf, B("bdt")], [B(f"dt{c}")], "dt_exp"))
            o.append(lambda c=c: sop(
                lambda s, c=c: s.activation(dt_str[:, c, :], dt_str[:, c, :], AF.Ln, bias=1.0),
                [B(f"dt{c}")], [B(f"dt{c}")], "dt_ln"))
            o.append(lambda c=c, p=p, xact_str=xact_str: vop(
                lambda v, c=c, xact_str=xact_str: v.tensor_tensor(
                    xact_str[:, c, :], dt_str[:, c, :], xact_str[:, c, :], ALU.mult),
                [B(f"dt{c}"), B(f"xact{c}_{p}")], [B(f"xact{c}_{p}")], "u"))
        return ar, o

    def gen_PRO(st):
        if st >= NSTR:
            return []
        p = st % 2
        xact_str = xact2[p]
        o = []
        for tc in range(2):
            for r in range(2):
                for h in range(2):
                    hb = B("p7") if h == 0 else B("p7")
                    for cb in (2 * h, 2 * h + 1):
                        c = r * 4 + cb
                        o.append(lambda tc=tc, c=c, cb=cb, hb=hb: top(
                            lambda t, c=c, cb=cb, tc=tc: t.transpose(
                                p7[:, bass.ts(cb, 128)], dt_str[:, c, bass.ts(tc, 128)], identf[:, :]),
                            [B(f"dt{c}"), B("identf"), hb], [hb], "tmT"))
                    o.append(lambda r=r, h=h, hb=hb: sop(
                        lambda s, r=r, h=h: s.copy(out=dtTM[:, r * 512 + h * 256:r * 512 + (h + 1) * 256],
                                                   in_=p7[:, bass.ts(h, 256)]),
                        [hb], [B("dtTM")], "tmTe"))
            for r in range(2):
                o.append(lambda r=r: top(
                    lambda t, r=r: t.matmul(p7[:, :], trif[:, :], dtTM[:, bass.ts(r, 512)],
                                            start=True, stop=True),
                    [B("trif"), B("dtTM"), B("p7"), B("p7")], [B("p7"), B("p7")], "pcum"))
                o.append(lambda r=r, tc=tc: sop(
                    lambda s, r=r, tc=tc: s.copy(out=phs[tc][:, bass.ts(r, 512)], in_=p7[:, :]),
                    [B("p7"), B("p7")], [B(f"phs{tc}")], "pcume"))
            for r in range(2):
                for h in range(2):
                    hb = B("p7") if h == 0 else B("p7")
                    for cb in (2 * h, 2 * h + 1):
                        c = r * 4 + cb
                        o.append(lambda tc=tc, c=c, cb=cb, p=p, hb=hb, xact_str=xact_str: top(
                            lambda t, c=c, cb=cb, tc=tc, xact_str=xact_str: t.transpose(
                                p7[:, bass.ts(cb, 128)], xact_str[:, c, bass.ts(tc, 128)], identf[:, :]),
                            [B(f"xact{c}_{p}"), B("identf"), hb], [hb], "tmTu"))
                    o.append(lambda tc=tc, r=r, h=h, hb=hb: sop(
                        lambda s, tc=tc, r=r, h=h: s.copy(
                            out=uTM[tc][:, r * 512 + h * 256:r * 512 + (h + 1) * 256],
                            in_=p7[:, bass.ts(h, 256)]),
                        [hb], [B(f"uTM{tc}")], "tmTue"))
        return o

    def gen_C(st):
        o = []
        p = st % 2
        sz_str = sz_str2[p]
        szb = B(f"sz_str{p}")
        xdb = xdb2[p]
        ychm = xact2[p]
        blk = st // 8
        win_i = st % 8
        for c in range(NCH):
            o.append(lambda c=c, p=p, xdb=xdb, ychm=ychm: vop(
                lambda v, c=c, xdb=xdb, ychm=ychm: v.tensor_tensor(
                    qt_t2[c % 2][:, :], xdb[:, c, :], ychm[:, c, :], ALU.add),
                [B(f"xdb{c}_{p}"), B(f"xact{c}_{p}")], [B(f"qt{c % 2}")], "qt"))
            o.append(lambda c=c, sz_str=sz_str, szb=szb: vop(
                lambda v, c=c, sz_str=sz_str: v.tensor_tensor(q_str[:, c, :], qt_t2[c % 2][:, :],
                                                              sz_str[:, c, :], ALU.mult),
                [B(f"qt{c % 2}"), szb], [B("q_str")], "q"))
        for dcol in range(NDC):
            pb, pbuf = (p0, B("p0")) if dcol % 2 == 0 else (p1, B("p1"))
            af = attf2[dcol % 2]
            afb = B(f"attf{dcol % 2}")
            o.append(lambda dcol=dcol, pb=pb, pbuf=pbuf: top(
                lambda t, dcol=dcol, pb=pb: [
                    t.matmul(pb[:, 0:STR], w_out_s[:, c, bass.ts(dcol, 128)], q_str[:, c, :],
                             start=(c == 0), stop=(c == NCH - 1)) for c in range(NCH)][-1],
                [B("wouts"), B("q_str"), pbuf], [pbuf], "wout_mm"))
            o.append(lambda dcol=dcol, pb=pb, pbuf=pbuf, af=af, afb=afb: vop(
                lambda v, dcol=dcol, pb=pb, af=af: v.tensor_scalar(
                    af[:, :], pb[:, 0:STR], gacol[:, dcol:dcol + 1], None, op0=ALU.mult),
                [pbuf, B("gacol")], [afb], "ga"))
            for ssub in range(2):
                o.append(lambda ssub=ssub, af=af, afb=afb: top(
                    lambda t, ssub=ssub, af=af: t.transpose(p6b[:, bass.ts(ssub, 128)],
                                                            af[:, bass.ts(ssub, 128)], identb[:, :]),
                    [afb, B("identb")], [B("p6b")], "attT"))
            o.append(lambda dcol=dcol: sop(
                lambda s, dcol=dcol: s.copy(
                    out=attnTM[:, :, bass.ts(dcol, 128)],
                    in_=p6b[:, 0:256].rearrange("p (a b) -> p a b", a=2)),
                [B("p6b")], [B("attnTM")], "attTM"))
            o.append(lambda dcol=dcol, win_i=win_i, blk=blk: dma(
                lambda e, win_i=win_i, blk=blk, dcol=dcol: e.dma_start(
                    out=att_in[win_i].ap()[blk][:, bass.ts(dcol, 128)].rearrange(
                        "(a p) d -> p a d", p=128),
                    in_=attnTM[:, :, bass.ts(dcol, 128)]),
                [B("attnTM")], [B(f"att_in{win_i}_{blk}")], "att_st"))
        return o

    def gen_RS(w):
        return lambda: fw.collective(
            lambda g, w=w: g.collective_compute("ReduceScatter", ALU.add, replica_groups=PAIRS,
                                                ins=[att_in[w].ap().opt()],
                                                outs=[att_out[w].ap().opt()]),
            reads=[B(f"att_in{w}_0"), B(f"att_in{w}_1")],
            writes=[B(f"att_out{w}")], tag="att_rs")

    for st in range(NSTR):
        p = st % 2
        for f_ in gen_A1(st):
            f_()
        _ar, _a2 = gen_A2(st)
        _ar()
        for f_ in _a2:
            f_()
        for f_ in gen_PRO(st):
            f_()
        for tc in range(2):
            g = st * 2 + tc
            for kk in range(KS + 2):
                if kk < KS:
                    k = kk
                    esl, wvi = k % 2, k % 4
                    sop(lambda s, k=k, esl=esl, tc=tc: s.activation(
                            egp[esl][:, :], phs[tc][:, :], AF.Exp, scale=nakb[:, k:k + 1]),
                        [B(f"phs{tc}"), B("nakb")], [B(f"eg{esl}")], "Ek")
                    for hf in range(2):
                        vop(lambda v, k=k, esl=esl, wvi=wvi, hf=hf, tc=tc, p=p: v.scalar_tensor_tensor(
                                wv[wvi][:, bass.ts(hf, 512)], egp[esl][:, bass.ts(hf, 512)],
                                BCt2[p][:, tc, k:k + 1], uTM[tc][:, bass.ts(hf, 512)],
                                ALU.mult, ALU.mult),
                            [B(f"eg{esl}"), B(f"BCt{p}"), B(f"uTM{tc}")], [B(f"wv{wvi}")], "vstt")
                    if g > 0:
                        fw.dma("gpsimd", (lambda k=k, wvi=wvi: (lambda e: e.dma_start(
                                out=wv[wvi][0:1, :], in_=crow[k:k + 1, :], accum_op=ALU.add)))(),
                               [B(f"crow{k}"), B(f"wv{wvi}")], [B(f"wv{wvi}")], "cadd")
                if 2 <= kk:
                    k = kk - 2
                    gsl, wti = 2 + (k % 2), k % 4
                    for hf in range(2):
                        vop(lambda v, gsl=gsl, wti=wti, hf=hf: v.tensor_tensor(
                                wt[wti][:, bass.ts(hf, 512)], egp[gsl][:, bass.ts(hf, 512)],
                                pHh[hf][:, :], ALU.mult),
                            [B(f"eg{gsl}"), B(f"pH{hf}")], [B(f"wt{wti}")], "ttt")
                    if g < 31:
                        fw.dma("sync", (lambda k=k, wti=wti: (lambda e: e.dma_start(
                                out=crow[k:k + 1, :], in_=wt[wti][127:128, :])))(),
                               [B(f"wt{wti}")], [B(f"crow{k}"), B("crowA")], "csave")
                    for hf in range(2):
                        top(lambda t, k=k, wti=wti, hf=hf, di=k % 2: t.matmul(
                                yTh[hf][:, :], diag[di][:, :], wt[wti][:, bass.ts(hf, 512)],
                                start=(k == 0), stop=(k == KS - 1)),
                            [B(f"diag{k % 2}"), B(f"wt{wti}"), B(f"yT{hf}")], [B(f"yT{hf}")], "red")
                if 1 <= kk <= KS:
                    k = kk - 1
                    gsl, wvi = 2 + (k % 2), k % 4
                    for hf in range(2):
                        reads = [B("trib"), B(f"wv{wvi}"), B(f"pH{hf}")]
                        top(lambda t, wvi=wvi, hf=hf: t.matmul(
                                pHh[hf][:, :], trib[:, :], wv[wvi][:, bass.ts(hf, 512)],
                                start=True, stop=True),
                            reads, [B(f"pH{hf}")], "tri")
                    sop(lambda s, k=k, gsl=gsl, tc=tc: s.activation(
                            egp[gsl][:, :], phs[tc][:, :], AF.Exp, scale=akb[:, k:k + 1]),
                        [B(f"phs{tc}"), B("akb")], [B(f"eg{gsl}")], "Gk")
                    sop(lambda s, k=k, tc=tc, di=k % 2, p=p: s.activation(
                            diag[di][:, :], identb[:, :], AF.Copy, scale=BCt2[p][:, tc, KS + k:KS + k + 1]),
                        [B("identb"), B(f"BCt{p}")], [B(f"diag{k % 2}")], "diagC")
            sop(lambda s: s.copy(out=ySB[:, 0:512], in_=yTh[0][:, :]),
                [B("yT0"), B("dtTM")], [B("dtTM")], "yev0")
            sop(lambda s: s.copy(out=ySB[:, 512:1024], in_=yTh[1][:, :]),
                [B("yT1"), B("dtTM")], [B("dtTM")], "yev1")
            for r in range(2):
                for h in range(2):
                    hb = B("p7") if h == 0 else B("p7")
                    for cb in (2 * h, 2 * h + 1):
                        c = r * 4 + cb
                        top(lambda t, cb=cb, c=c: t.transpose(
                                p7[:, bass.ts(cb, 128)], ySB[:, bass.ts(c, 128)], identf[:, :]),
                            [B("dtTM"), B("identf"), hb], [hb], "yT2")
                    sop(lambda s, r=r, h=h, tc=tc, p=p: s.copy(
                            out=xact2[p][:, 4 * r + 2 * h:4 * r + 2 * h + 2, bass.ts(tc, 128)],
                            in_=p7[:, bass.ts(h, 256)].rearrange("p (a b) -> p a b", a=2)),
                        [hb] + [B(f"xact{4 * r + 2 * h + i}_{p}") for i in range(2)],
                        [B(f"xact{4 * r + 2 * h + i}_{p}") for i in range(2)], "yT2e")
        for f_ in gen_C(st):
            f_()
        if st >= 8:
            gen_RS(st - 8)()


    for w in range(NW):
        for hw in range(2):
            r0 = w * 256 + hw * 128
            dma(lambda e, w=w, hw=hw: e.dma_start(
                    out=rs_t.ap(),
                    in_=att_out[w].ap()[bass.ts(hw, 128), :].rearrange("(a p) d -> p a d", p=128)),
                [B("rs_t"), B(f"att_out{w}")], [B("rs_t")], "rs_ld")
            dma(lambda e, r0=r0: e.dma_start(
                    out=uTM[1][:, :], in_=x_own.ap()[r0:r0 + 128, :]),
                [B("uTM1")], [B("uTM1")], "xo_ld")
            vop(lambda v: v.tensor_tensor(uTM[1][:, :], uTM[1][:, :], rs_t[:, 0, :], ALU.add),
                [B("uTM1"), B("rs_t")], [B("uTM1")], "xres")
            dma(lambda e, r0=r0: e.dma_start(
                    out=xres_b.ap()[r0:r0 + 128, :], in_=uTM[1][:, :]),
                [B("uTM1")], [B("xres_bd")], "xres_st")

    fw.phase_barrier()
    mam.close()

    ffn = contextlib.ExitStack()
    xr_t2 = [sb(f"xr_t{i}", [128, D], F32, ffn) for i in range(2)]
    st8_2 = [sb(f"st8_{i}", [128, 8], F32, ffn) for i in range(2)]
    mmacc = sb("mmacc", [128, 4], F32, ffn)
    mm2 = sb("mm2", [1, 2], F32, ffn)
    i32s = sb("i32s", [1, 1], I32, ffn)
    wld2 = [sb(f"wld{i}", [128, 512], F32, ffn) for i in range(2)]
    wi322 = [sb(f"wi32{i}", [128, 512], I32, ffn) for i in range(2)]
    wqb2 = [sb(f"wqb{i}", [128, 512], BF16, ffn) for i in range(2)]
    wq2T = sb("wq2T", [128, NHC, D], BF16, ffn)
    cs1 = sb("cs1", [128, NHC], F32, ffn)
    cs2 = sb("cs2", [128, NDC], F32, ffn)
    be1 = sb("be1", [128, NHC], F32, ffn)
    be2 = sb("be2", [128, NDC], F32, ffn)
    b1c_t = sb("b1c_t", [128, NHC], F32, ffn)
    b2c_t = sb("b2c_t", [128, NDC], F32, ffn)
    g_t2 = [sb(f"g_t{i}", [128, 512], F32, ffn) for i in range(2)]
    red12 = [sb(f"red1{i}", [128, 1], F32, ffn) for i in range(2)]
    ffnA = contextlib.ExitStack()
    h2_t2 = [sb(f"h2_t{i}", [128, D], F32, ffnA) for i in range(2)]
    lst = sb("lst", [128, 2 * (SH // 128)], F32, ffnA)
    capb_bc = sb("capb_bc", [128, D], F32, ffnA)
    sb_bc = sb("sb_bc", [128, D], F32, ffnA)
    i32t2 = [sb(f"i32t{i}", [128, D], I32, ffnA) for i in range(2)]
    xq_tm2 = [sb(f"xq_tm{i}", [128, D], BF16, ffnA) for i in range(2)]
    xqT = sb("xqT", [128, NDC, SH], BF16, ffnA)
    wq1c2 = [sb(f"wq1c{i}", [128, NDC, 128], BF16, ffnA) for i in range(2)]

    dma(lambda e: e.dma_start(out=b1c_t.ap(), in_=b1_c.ap()), [], [B("b1c")])
    dma(lambda e: e.dma_start(out=b2c_t.ap(), in_=b2_c.ap()), [], [B("b2c")])
    vop(lambda v: v.tensor_scalar(i32s[0:1, 0:1], constc[0:1, 3:4], 1.0, None, op0=ALU.mult),
        [B("constc")], [B("i32s")], "probe")
    vop(lambda v: v.tensor_scalar(scal[0:1, 22:23], i32s[0:1, 0:1], -0.5, 0.5,
                                  op0=ALU.mult, op1=ALU.add),
        [B("i32s")], [B("scal")], "moff")
    vop(lambda v: v.tensor_scalar(scal[0:1, 23:24], scal[0:1, 22:23], 256.0, None, op0=ALU.add),
        [B("scal")], [B("scal")], "rndc")
    top(lambda t: t.matmul(p7[:, 0:1], onesrf[:, :], scal[0:1, 23:24], start=True, stop=True),
        [B("scal"), B("onesrf"), B("p7")], [B("p7")], "rndc_bc")
    sop(lambda s: s.copy(out=scalb[:, 10:11], in_=p7[:, 0:1]), [B("p7")], [B("scalb")], "rndc_c")
    vop(lambda v: v.memset(mmacc[:, 0:1], -3.0e38), [], [B("mmacc")])
    vop(lambda v: v.memset(mmacc[:, 1:2], 3.0e38), [B("mmacc")], [B("mmacc")])
    vop(lambda v: v.memset(mmacc[:, 2:3], -3.0e38), [B("mmacc")], [B("mmacc")])
    vop(lambda v: v.memset(mmacc[:, 3:4], 3.0e38), [B("mmacc")], [B("mmacc")])

    rowst = sb("rowst", [1, 128], F32, ffnA)
    for (base, dstb, dst, extra) in ((32, "capb_bc", capb_bc, 1.0), (24, "sb_bc", sb_bc, 0.0)):
        for i in range(NDC):
            top(lambda t, base=base, i=i: t.transpose(p7[0:1, 0:128], modT[:, base + i:base + i + 1],
                                                      identf[:, :]),
                [B("modT"), B("identf"), B("p7")], [B("p7")], "cbT")
            sop(lambda s: s.copy(out=rowst[:, :], in_=p7[0:1, 0:128]), [B("p7")], [B("rowst")])
            top(lambda t: t.matmul(p7[:, 0:128], onesrf[:, :], rowst[:, :], start=True, stop=True),
                [B("onesrf"), B("rowst"), B("p7")], [B("p7")], "cbbc")
            if extra:
                vop(lambda v, i=i, dst=dst: v.tensor_scalar(dst[:, bass.ts(i, 128)], p7[:, 0:128],
                                                            1.0, None, op0=ALU.add),
                    [B("p7")], [B(dstb)])
            else:
                sop(lambda s, i=i, dst=dst: s.copy(out=dst[:, bass.ts(i, 128)], in_=p7[:, 0:128]),
                    [B("p7")], [B(dstb)])

    for (wten, col, tag) in ((w1t, 0, "w1a"), (w2t, 1, "w2a")):
        n_r, n_c = (NDC, 8) if col == 0 else (NHC, 2)
        first = True
        for rr in range(n_r):
            for cq in range(n_c):
                bi = (rr * n_c + cq) % 2
                wld, red1 = wld2[bi], red12[bi]
                dma(lambda e, rr=rr, cq=cq, wld=wld, wten=wten: e.dma_start(
                        out=wld[:, :], in_=wten.ap()[bass.ts(rr, 128), bass.ts(cq, 512)]),
                    [B(f"wld{bi}")], [B(f"wld{bi}")], tag + "_ld")
                vop(lambda v, wld=wld, red1=red1: v.tensor_reduce(
                        red1[:, :], wld[:, :], AX.X, ALU.max, apply_absolute_value=True),
                    [B(f"wld{bi}")], [B(f"red1{bi}")], tag + "_r")
                if first:
                    vop(lambda v, red1=red1, col=col: v.tensor_copy(st8_2[0][:, col:col + 1], red1[:, :]),
                        [B(f"red1{bi}")], [B("st8_0")], tag + "_a0")
                    first = False
                else:
                    vop(lambda v, red1=red1, col=col: v.tensor_tensor(
                            st8_2[0][:, col:col + 1], st8_2[0][:, col:col + 1], red1[:, :], ALU.max),
                        [B(f"red1{bi}"), B("st8_0")], [B("st8_0")], tag + "_a")
        top(lambda t, col=col: t.transpose(p7[0:1, 0:128], st8_2[0][:, col:col + 1], identf[:, :]),
            [B("st8_0"), B("identf"), B("p7")], [B("p7")])
        vop(lambda v, col=col: v.tensor_reduce(scal[0:1, 16 + col:17 + col], p7[0:1, 0:128], AX.X, ALU.max),
            [B("p7")], [B("scal")])
    vop(lambda v: v.tensor_scalar(scal[0:1, 5:6], scal[0:1, 16:17], 1.0 / 127.0, 1e-8,
                                  op0=ALU.mult, op1=ALU.max), [B("scal")], [B("scal")])
    vop(lambda v: v.tensor_scalar(scal[0:1, 13:14], scal[0:1, 17:18], 1.0 / 127.0, 1e-8,
                                  op0=ALU.mult, op1=ALU.max), [B("scal")], [B("scal")])
    vop(lambda v: v.reciprocal(scal[0:1, 6:7], scal[0:1, 5:6]), [B("scal")], [B("scal")])
    vop(lambda v: v.reciprocal(scal[0:1, 14:15], scal[0:1, 13:14]), [B("scal")], [B("scal")])
    for (srcslot, col) in ((6, 8), (14, 9)):
        top(lambda t, srcslot=srcslot: t.matmul(p7[:, 0:1], onesrf[:, :], scal[0:1, srcslot:srcslot + 1],
                                                start=True, stop=True),
            [B("scal"), B("onesrf"), B("p7")], [B("p7")], "rswbc")
        sop(lambda s, col=col: s.copy(out=scalb[:, col:col + 1], in_=p7[:, 0:1]),
            [B("p7")], [B("scalb")])

    for hg in range(8):
        for dc in range(NDC):
            bi = (hg * NDC + dc) % 2
            wld, wi32, wqb = wld2[bi], wi322[bi], wqb2[bi]
            dma(lambda e, hg=hg, dc=dc, wld=wld: e.dma_start(
                    out=wld[:, :], in_=w1t.ap()[bass.ts(dc, 128), bass.ts(hg, 512)]),
                [B(f"wld{bi}")], [B(f"wld{bi}")], "w1q_ld")
            vop(lambda v, wld=wld, wi32=wi32: v.tensor_scalar(
                    wi32[:, :], wld[:, :], scalb[:, 8:9], scalb[:, 10:11],
                    op0=ALU.mult, op1=ALU.add),
                [B(f"wld{bi}"), B("scalb")], [B(f"wi32{bi}")], "w1q_i")
            vop(lambda v, wld=wld, wi32=wi32: v.tensor_scalar(
                    wld[:, :], wi32[:, :], 256.0 + 127.0, 256.0 - 127.0, op0=ALU.min, op1=ALU.max),
                [B(f"wi32{bi}")], [B(f"wld{bi}")], "w1q_c")
            vop(lambda v, wld=wld, wqb=wqb: v.tensor_scalar(
                    wqb[:, :], wld[:, :], -256.0, None, op0=ALU.add),
                [B(f"wld{bi}")], [B(f"wqb{bi}")], "w1q_s")
            dma(lambda e, hg=hg, dc=dc, wqb=wqb: e.dma_start(
                    out=wq1_b.ap()[bass.ts(dc, 128), bass.ts(hg, 512)], in_=wqb[:, :]),
                [B(f"wqb{bi}")], [B("wq1_bd")], "w1q_st")
            csb = [pHh[0], pHh[1], yTh[0], yTh[1]]
            top(lambda t, dc=dc, wqb=wqb, csb=csb: [
                    t.matmul(csb[q][:, 0:1], wqb[:, bass.ts(q, 128)], onescb[:, :],
                             start=(dc == 0), stop=(dc == NDC - 1)) for q in range(4)][-1],
                [B(f"wqb{bi}"), B("onescb"), B("pH0"), B("pH1"), B("yT0"), B("yT1")],
                [B("pH0"), B("pH1"), B("yT0"), B("yT1")], "cs1mm")
        for q in range(4):
            csb = [pHh[0], pHh[1], yTh[0], yTh[1]]
            sop(lambda s, hg=hg, q=q, csb=csb: s.copy(out=cs1[:, hg * 4 + q:hg * 4 + q + 1],
                                                      in_=csb[q][:, 0:1]),
                [B("pH0"), B("pH1"), B("yT0"), B("yT1")], [B("cs1")], "cs1ev")

    for hc in range(NHC):
        for g2 in range(2):
            bi = (hc * 2 + g2) % 2
            wld, wi32 = wld2[bi], wi322[bi]
            dma(lambda e, hc=hc, g2=g2, wld=wld: e.dma_start(
                    out=wld[:, :], in_=w2t.ap()[bass.ts(hc, 128), bass.ts(g2, 512)]),
                [B(f"wld{bi}")], [B(f"wld{bi}")], "w2q_ld")
            vop(lambda v, wld=wld, wi32=wi32: v.tensor_scalar(
                    wi32[:, :], wld[:, :], scalb[:, 9:10], scalb[:, 10:11],
                    op0=ALU.mult, op1=ALU.add),
                [B(f"wld{bi}"), B("scalb")], [B(f"wi32{bi}")], "w2q_i")
            vop(lambda v, wld=wld, wi32=wi32: v.tensor_scalar(
                    wld[:, :], wi32[:, :], 256.0 + 127.0, 256.0 - 127.0, op0=ALU.min, op1=ALU.max),
                [B(f"wi32{bi}")], [B(f"wld{bi}")], "w2q_c")
            vop(lambda v, hc=hc, g2=g2, wld=wld: v.tensor_scalar(
                    wq2T[:, hc, bass.ts(g2, 512)], wld[:, :], -256.0, None, op0=ALU.add),
                [B(f"wld{bi}")], [B("wq2T")], "w2q_s")
    for dcol in range(NDC):
        top(lambda t, dcol=dcol: [
                t.matmul(p7[:, NHC + dcol:NHC + dcol + 1], wq2T[:, hc, bass.ts(dcol, 128)],
                         onescb[:, :], start=(hc == 0), stop=(hc == NHC - 1))
                for hc in range(NHC)][-1],
            [B("wq2T"), B("onescb"), B("p7")], [B("p7")], "cs2mm")
    sop(lambda s: s.copy(out=cs2[:, :], in_=p7[:, NHC:NHC + NDC]), [B("p7")], [B("cs2")])

    NT = SH // 128
    for t_i in range(NT):
        bi = t_i % 2
        xr_t, h2_t, st8, red1 = xr_t2[bi], h2_t2[bi], st8_2[bi], red12[bi]
        xrb, h2b, stb, rdb = B(f"xr_t{bi}"), B(f"h2_t{bi}"), B(f"st8_{bi}"), B(f"red1{bi}")
        dma(lambda e, t_i=t_i, xr_t=xr_t: e.dma_start(out=xr_t[:, :], in_=xres_b.ap()[bass.ts(t_i, 128), :]),
            [xrb, B("xres_bd")], [xrb], "pa_ld")
        vop(lambda v, xr_t=xr_t, st8=st8: v.tensor_reduce(st8[:, 0:1], xr_t[:, :], AX.X, ALU.add),
            [xrb], [stb])
        sop(lambda s, xr_t=xr_t, h2_t=h2_t, st8=st8: s.activation(h2_t[:, :], xr_t[:, :], AF.Square,
                                                                  accum_out=st8[:, 1:2]),
            [xrb], [h2b, stb])
        vop(lambda v, st8=st8: v.tensor_scalar(st8[:, 2:3], st8[:, 0:1], 1.0 / D, None, op0=ALU.mult),
            [stb], [stb])
        sop(lambda s, st8=st8: s.activation(st8[:, 3:4], st8[:, 2:3], AF.Square), [stb], [stb])
        vop(lambda v, st8=st8: v.tensor_scalar(st8[:, 4:5], st8[:, 1:2], 1.0 / D, st8[:, 3:4],
                                               op0=ALU.mult, op1=ALU.subtract), [stb], [stb])
        sop(lambda s, st8=st8: s.activation(st8[:, 5:6], st8[:, 4:5], AF.Sqrt, bias=1e-5),
            [stb], [stb])
        vop(lambda v, st8=st8: v.reciprocal(st8[:, 6:7], st8[:, 5:6]), [stb], [stb])
        vop(lambda v, t_i=t_i, st8=st8: v.tensor_copy(lst[:, 2 * t_i:2 * t_i + 1], st8[:, 2:3]),
            [stb], [B("lst")])
        vop(lambda v, t_i=t_i, st8=st8: v.tensor_copy(lst[:, 2 * t_i + 1:2 * t_i + 2], st8[:, 6:7]),
            [stb], [B("lst")])
        vop(lambda v, xr_t=xr_t, h2_t=h2_t, st8=st8: v.tensor_scalar(
                h2_t[:, :], xr_t[:, :], st8[:, 2:3], st8[:, 6:7], op0=ALU.subtract, op1=ALU.mult),
            [xrb, stb], [h2b])
        vop(lambda v, h2_t=h2_t: v.tensor_tensor(h2_t[:, :], h2_t[:, :], capb_bc[:, :], ALU.mult),
            [h2b, B("capb_bc")], [h2b])
        vop(lambda v, h2_t=h2_t: v.tensor_tensor(h2_t[:, :], h2_t[:, :], sb_bc[:, :], ALU.add),
            [h2b, B("sb_bc")], [h2b])
        vop(lambda v, h2_t=h2_t, red1=red1: v.tensor_reduce(red1[:, :], h2_t[:, :], AX.X, ALU.max),
            [h2b], [rdb])
        vop(lambda v, red1=red1: v.tensor_tensor(mmacc[:, 0:1], mmacc[:, 0:1], red1[:, :], ALU.max),
            [rdb, B("mmacc")], [B("mmacc")])
        vop(lambda v, h2_t=h2_t, red1=red1: v.tensor_reduce(red1[:, :], h2_t[:, :], AX.X, ALU.min),
            [h2b], [rdb])
        vop(lambda v, red1=red1: v.tensor_tensor(mmacc[:, 1:2], mmacc[:, 1:2], red1[:, :], ALU.min),
            [rdb, B("mmacc")], [B("mmacc")])
    top(lambda t: t.transpose(p7[0:1, 0:128], mmacc[:, 0:1], identf[:, :]),
        [B("mmacc"), B("identf")], [B("p7")])
    vop(lambda v: v.tensor_reduce(mm2[0:1, 0:1], p7[0:1, 0:128], AX.X, ALU.max),
        [B("p7")], [B("mm2")])
    top(lambda t: t.transpose(p7[0:1, 0:128], mmacc[:, 1:2], identf[:, :]),
        [B("mmacc"), B("identf"), B("p7")], [B("p7")])
    vop(lambda v: v.tensor_reduce(mm2[0:1, 1:2], p7[0:1, 0:128], AX.X, ALU.min),
        [B("p7")], [B("mm2")])
    vop(lambda v: v.tensor_scalar(mm2[0:1, 1:2], mm2[0:1, 1:2], -1.0, None, op0=ALU.mult),
        [B("mm2")], [B("mm2")])
    dma(lambda e: e.dma_start(out=mmx_in.ap(), in_=mm2[0:1, :]), [B("mm2")], [B("mmx_in")], "mmx_st")
    fw.collective(lambda g: g.collective_compute("AllReduce", ALU.max, replica_groups=ALL8,
                                                 ins=[mmx_in.ap().opt()], outs=[mmx_out.ap().opt()]),
                  reads=[B("mmx_in")], writes=[B("mmx_out")], tag="mmx_ar")
    dma(lambda e: e.dma_start(out=mm2[0:1, :], in_=mmx_out.ap()),
        [B("mm2"), B("mmx_out")], [B("mm2")], "mmx_ld")

    vop(lambda v: v.tensor_tensor(scal[0:1, 20:21], mm2[0:1, 0:1], mm2[0:1, 1:2], ALU.add),
        [B("mm2")], [B("scal")], "q1_rng")
    vop(lambda v: v.tensor_scalar(scal[0:1, 2:3], scal[0:1, 20:21], 1.0 / 255.0, 1e-8,
                                  op0=ALU.mult, op1=ALU.max), [B("scal")], [B("scal")], "q1_s")
    vop(lambda v: v.reciprocal(scal[0:1, 3:4], scal[0:1, 2:3]), [B("scal")], [B("scal")], "q1_rs")
    vop(lambda v: v.tensor_tensor(scal[0:1, 18:19], mm2[0:1, 1:2], scal[0:1, 3:4], ALU.mult),
        [B("mm2"), B("scal")], [B("scal")], "q1_z0")
    sop(lambda s: s.activation(i32s[0:1, 0:1], scal[0:1, 18:19], AF.Copy, bias=RND),
        [B("scal"), B("constc")], [B("i32s")], "q1_zi")
    vop(lambda v: v.tensor_scalar(scal[0:1, 18:19], i32s[0:1, 0:1], 256.0 + 255.0, 256.0,
                                  op0=ALU.min, op1=ALU.max), [B("i32s")], [B("scal")], "q1_zc")
    vop(lambda v: v.tensor_scalar(scal[0:1, 4:5], scal[0:1, 18:19], -256.0 - 128.0, None,
                                  op0=ALU.add), [B("scal")], [B("scal")], "q1_zs")
    vop(lambda v: v.tensor_tensor(scal[0:1, 7:8], scal[0:1, 2:3], scal[0:1, 5:6], ALU.mult),
        [B("scal")], [B("scal")], "q1_sx")
    vop(lambda v: v.tensor_tensor(scal[0:1, 19:20], scal[0:1, 4:5], scal[0:1, 7:8], ALU.mult),
        [B("scal")], [B("scal")], "q1_zx")
    vop(lambda v: v.tensor_scalar(scal[0:1, 24:25], scal[0:1, 4:5], 1.0, scal[0:1, 23:24],
                                  op0=ALU.mult, op1=ALU.add),
        [B("scal")], [B("scal")], "q1_zpr")
    for (srcslot, col, extra) in ((3, 0, 0.0), (24, 1, 0.0), (7, 2, 0.0), (19, 3, 0.0)):
        top(lambda t, srcslot=srcslot: t.matmul(p7[:, 0:1], onesrf[:, :],
                                                scal[0:1, srcslot:srcslot + 1],
                                                start=True, stop=True),
            [B("scal"), B("onesrf"), B("p7")], [B("p7")], "q1_bc")
        if extra:
            vop(lambda v, col=col, extra=extra: v.tensor_scalar(
                    scalb[:, col:col + 1], p7[:, 0:1], extra, None, op0=ALU.add),
                [B("p7")], [B("scalb")], "q1_bca")
        else:
            sop(lambda s, col=col: s.copy(out=scalb[:, col:col + 1], in_=p7[:, 0:1]),
                [B("p7")], [B("scalb")], "q1_bcc")
    vop(lambda v: v.scalar_tensor_tensor(be1[:, :], cs1[:, :], scalb[:, 3:4], b1c_t[:, :],
                                         ALU.mult, ALU.subtract),
        [B("cs1"), B("scalb"), B("b1c")], [B("be1")], "be1")
    vop(lambda v: v.tensor_scalar(be1[:, :], be1[:, :], -1.0, None, op0=ALU.mult),
        [B("be1")], [B("be1")], "be1n")

    for t_i in range(NT):
        bi = t_i % 2
        xr_t, h2_t, i32t, xq_tm = xr_t2[bi], h2_t2[bi], i32t2[bi], xq_tm2[bi]
        xrb, h2b, i3b, xqb = B(f"xr_t{bi}"), B(f"h2_t{bi}"), B(f"i32t{bi}"), B(f"xq_tm{bi}")
        dma(lambda e, t_i=t_i, xr_t=xr_t: e.dma_start(out=xr_t[:, :], in_=xres_b.ap()[bass.ts(t_i, 128), :]),
            [xrb, B("xres_bd")], [xrb], "pb_ld")
        vop(lambda v, t_i=t_i, xr_t=xr_t, h2_t=h2_t: v.tensor_scalar(
                h2_t[:, :], xr_t[:, :], lst[:, 2 * t_i:2 * t_i + 1],
                lst[:, 2 * t_i + 1:2 * t_i + 2], op0=ALU.subtract, op1=ALU.mult),
            [xrb, B("lst")], [h2b])
        vop(lambda v, h2_t=h2_t: v.tensor_tensor(h2_t[:, :], h2_t[:, :], capb_bc[:, :], ALU.mult),
            [h2b, B("capb_bc")], [h2b])
        vop(lambda v, h2_t=h2_t: v.tensor_tensor(h2_t[:, :], h2_t[:, :], sb_bc[:, :], ALU.add),
            [h2b, B("sb_bc")], [h2b])
        vop(lambda v, h2_t=h2_t, i32t=i32t: v.tensor_scalar(
                i32t[:, :], h2_t[:, :], scalb[:, 0:1], scalb[:, 1:2], op0=ALU.mult, op1=ALU.add),
            [h2b, B("scalb")], [i3b], "xq_i")
        vop(lambda v, h2_t=h2_t, i32t=i32t: v.tensor_scalar(
                h2_t[:, :], i32t[:, :], 256.0 + 127.0, 256.0 - 128.0, op0=ALU.min, op1=ALU.max),
            [i3b], [h2b], "xq_c")
        vop(lambda v, h2_t=h2_t, xq_tm=xq_tm: v.tensor_scalar(
                xq_tm[:, :], h2_t[:, :], -256.0, None, op0=ALU.add),
            [h2b], [xqb], "xq_s")
        for dc in range(NDC):
            top(lambda t, dc=dc, xq_tm=xq_tm: t.transpose(
                    p6b[:, bass.ts(dc, 128)], xq_tm[:, bass.ts(dc, 128)], identb[:, :]),
                [xqb, B("identb"), B("p6b")], [B("p6b")], "xqT")
        sop(lambda s, t_i=t_i: s.copy(out=xqT[:, :, bass.ts(t_i, 128)],
                                      in_=p6b.ap().rearrange("p (c n) -> p c n", c=NDC)),
            [B("p6b")], [B("xqT")], "xqTc")

    for hc in range(NHC):
        wq1c = wq1c2[hc % 2]
        wqcb = B(f"wq1c{hc % 2}")
        dma(lambda e, hc=hc, wq1c=wq1c: e.dma_start(
                out=wq1c.ap(),
                in_=wq1_b.ap().rearrange("(c p) h -> p c h", p=128)[:, :, bass.ts(hc, 128)]),
            [wqcb, B("wq1_bd")], [wqcb], "f1_wld")
        for w in range(FW):
            pb, pbuf = (p0, B("p0")) if w % 2 == 0 else (p1, B("p1"))
            bi = (hc * FW + w) % 2
            g_t, red1 = g_t2[bi], red12[bi]
            gb_, rdb = B(f"g_t{bi}"), B(f"red1{bi}")
            top(lambda t, hc=hc, w=w, pb=pb, wq1c=wq1c: [
                    t.matmul(pb[:, :], wq1c[:, dc, :], xqT[:, dc, bass.ts(w, 512)],
                             start=(dc == 0), stop=(dc == NDC - 1)) for dc in range(NDC)][-1],
                [wqcb, B("xqT"), pbuf], [pbuf], "f1_mm")
            sop(lambda s, hc=hc, pb=pb, g_t=g_t: s.activation(g_t[:, :], pb[:, :], AF.Gelu,
                                                     scale=scalb[:, 2:3], bias=be1[:, hc:hc + 1]),
                [pbuf, B("scalb"), B("be1")], [gb_], "gelu")
            dma(lambda e, hc=hc, w=w, g_t=g_t: e.dma_start(
                    out=g_b.ap()[bass.ts(hc, 128), bass.ts(w, 512)], in_=g_t[:, :]),
                [gb_], [B("g_bd")], "g_st")
            vop(lambda v, g_t=g_t, red1=red1: v.tensor_reduce(red1[:, :], g_t[:, :], AX.X, ALU.max),
                [gb_], [rdb])
            vop(lambda v, red1=red1: v.tensor_tensor(mmacc[:, 2:3], mmacc[:, 2:3], red1[:, :], ALU.max),
                [rdb, B("mmacc")], [B("mmacc")])
            vop(lambda v, g_t=g_t, red1=red1: v.tensor_reduce(red1[:, :], g_t[:, :], AX.X, ALU.min),
                [gb_], [rdb])
            vop(lambda v, red1=red1: v.tensor_tensor(mmacc[:, 3:4], mmacc[:, 3:4], red1[:, :], ALU.min),
                [rdb, B("mmacc")], [B("mmacc")])
    top(lambda t: t.transpose(p7[0:1, 0:128], mmacc[:, 2:3], identf[:, :]),
        [B("mmacc"), B("identf"), B("p7")], [B("p7")])
    vop(lambda v: v.tensor_reduce(mm2[0:1, 0:1], p7[0:1, 0:128], AX.X, ALU.max),
        [B("p7"), B("mm2")], [B("mm2")])
    top(lambda t: t.transpose(p7[0:1, 0:128], mmacc[:, 3:4], identf[:, :]),
        [B("mmacc"), B("identf"), B("p7")], [B("p7")])
    vop(lambda v: v.tensor_reduce(mm2[0:1, 1:2], p7[0:1, 0:128], AX.X, ALU.min),
        [B("p7")], [B("mm2")])
    vop(lambda v: v.tensor_scalar(mm2[0:1, 1:2], mm2[0:1, 1:2], -1.0, None, op0=ALU.mult),
        [B("mm2")], [B("mm2")])
    dma(lambda e: e.dma_start(out=gmx_in.ap(), in_=mm2[0:1, :]), [B("mm2")], [B("gmx_in")], "gmx_st")
    fw.collective(lambda g: g.collective_compute("AllReduce", ALU.max, replica_groups=ALL8,
                                                 ins=[gmx_in.ap().opt()], outs=[gmx_out.ap().opt()]),
                  reads=[B("gmx_in")], writes=[B("gmx_out")], tag="gmx_ar")
    dma(lambda e: e.dma_start(out=mm2[0:1, :], in_=gmx_out.ap()),
        [B("mm2"), B("gmx_out")], [B("mm2")], "gmx_ld")
    vop(lambda v: v.tensor_tensor(scal[0:1, 20:21], mm2[0:1, 0:1], mm2[0:1, 1:2], ALU.add),
        [B("mm2")], [B("scal")], "q2_rng")
    vop(lambda v: v.tensor_scalar(scal[0:1, 10:11], scal[0:1, 20:21], 1.0 / 255.0, 1e-8,
                                  op0=ALU.mult, op1=ALU.max), [B("scal")], [B("scal")], "q2_s")
    vop(lambda v: v.reciprocal(scal[0:1, 11:12], scal[0:1, 10:11]), [B("scal")], [B("scal")], "q2_rs")
    vop(lambda v: v.tensor_tensor(scal[0:1, 18:19], mm2[0:1, 1:2], scal[0:1, 11:12], ALU.mult),
        [B("mm2"), B("scal")], [B("scal")], "q2_z0")
    sop(lambda s: s.activation(i32s[0:1, 0:1], scal[0:1, 18:19], AF.Copy, bias=RND),
        [B("scal"), B("constc")], [B("i32s")], "q2_zi")
    vop(lambda v: v.tensor_scalar(scal[0:1, 18:19], i32s[0:1, 0:1], 256.0 + 255.0, 256.0,
                                  op0=ALU.min, op1=ALU.max), [B("i32s")], [B("scal")], "q2_zc")
    vop(lambda v: v.tensor_scalar(scal[0:1, 12:13], scal[0:1, 18:19], -256.0 - 128.0, None,
                                  op0=ALU.add), [B("scal")], [B("scal")], "q2_zs")
    vop(lambda v: v.tensor_tensor(scal[0:1, 15:16], scal[0:1, 10:11], scal[0:1, 13:14], ALU.mult),
        [B("scal")], [B("scal")], "q2_sx")
    vop(lambda v: v.tensor_tensor(scal[0:1, 19:20], scal[0:1, 12:13], scal[0:1, 15:16], ALU.mult),
        [B("scal")], [B("scal")], "q2_zx")
    vop(lambda v: v.tensor_scalar(scal[0:1, 25:26], scal[0:1, 12:13], 1.0, scal[0:1, 23:24],
                                  op0=ALU.mult, op1=ALU.add),
        [B("scal")], [B("scal")], "q2_zpr")
    for (srcslot, col, extra) in ((11, 4, 0.0), (25, 5, 0.0), (15, 6, 0.0), (19, 7, 0.0)):
        top(lambda t, srcslot=srcslot: t.matmul(p7[:, 0:1], onesrf[:, :],
                                                scal[0:1, srcslot:srcslot + 1],
                                                start=True, stop=True),
            [B("scal"), B("onesrf"), B("p7")], [B("p7")], "q2_bc")
        if extra:
            vop(lambda v, col=col, extra=extra: v.tensor_scalar(
                    scalb[:, col:col + 1], p7[:, 0:1], extra, None, op0=ALU.add),
                [B("p7")], [B("scalb")], "q2_bca")
        else:
            sop(lambda s, col=col: s.copy(out=scalb[:, col:col + 1], in_=p7[:, 0:1]),
                [B("p7")], [B("scalb")], "q2_bcc")
    vop(lambda v: v.scalar_tensor_tensor(be2[:, :], cs2[:, :], scalb[:, 7:8], b2c_t[:, :],
                                         ALU.mult, ALU.subtract),
        [B("cs2"), B("scalb"), B("b2c")], [B("be2")], "be2")
    vop(lambda v: v.tensor_scalar(be2[:, :], be2[:, :], -1.0, None, op0=ALU.mult),
        [B("be2")], [B("be2")], "be2n")

    fw.phase_barrier()
    ffnA.close()
    ffnB = contextlib.ExitStack()
    gq_t = sb("gq_t", [128, NHC, 512], BF16, ffnB)
    gi322 = [sb(f"gi32{i}", [128, 512], I32, ffnB) for i in range(2)]
    e2_t = sb("e2_t", [128, NDC, 512], F32, ffnB)
    out_t = sb("out_t", [128, D], F32, ffnB)
    for w in range(FW):
        for hc in range(NHC):
            bi = hc % 2
            g_t, gi32 = g_t2[bi], gi322[bi]
            gb_, gib = B(f"g_t{bi}"), B(f"gi32{bi}")
            dma(lambda e, hc=hc, w=w, g_t=g_t: e.dma_start(
                    out=g_t[:, :], in_=g_b.ap()[bass.ts(hc, 128), bass.ts(w, 512)]),
                [gb_, B("g_bd")], [gb_], "f2_gld")
            vop(lambda v, g_t=g_t, gi32=gi32: v.tensor_scalar(
                    gi32[:, :], g_t[:, :], scalb[:, 4:5], scalb[:, 5:6], op0=ALU.mult, op1=ALU.add),
                [gb_, B("scalb")], [gib], "gq_i")
            vop(lambda v, g_t=g_t, gi32=gi32: v.tensor_scalar(
                    g_t[:, :], gi32[:, :], 256.0 + 127.0, 256.0 - 128.0, op0=ALU.min, op1=ALU.max),
                [gib], [gb_], "gq_c")
            vop(lambda v, hc=hc, g_t=g_t: v.tensor_scalar(gq_t[:, hc, :], g_t[:, :], -256.0, None,
                                                 op0=ALU.add),
                [gb_], [B("gq_t")], "gq_s")
        for dcol in range(NDC):
            pb, pbuf = (p0, B("p0")) if dcol % 2 == 0 else (p1, B("p1"))
            top(lambda t, dcol=dcol, pb=pb: [
                    t.matmul(pb[:, :], wq2T[:, hc, bass.ts(dcol, 128)], gq_t[:, hc, :],
                             start=(hc == 0), stop=(hc == NHC - 1)) for hc in range(NHC)][-1],
                [B("wq2T"), B("gq_t"), pbuf], [pbuf], "f2_mm")
            vop(lambda v, dcol=dcol, pb=pb: v.tensor_scalar(
                    e2_t[:, dcol, :], pb[:, :], scalb[:, 6:7], be2[:, dcol:dcol + 1],
                    op0=ALU.mult, op1=ALU.add),
                [pbuf, B("scalb"), B("be2")], [B("e2_t")], "f2_ev")
            vop(lambda v, dcol=dcol: v.tensor_scalar(
                    e2_t[:, dcol, :], e2_t[:, dcol, :], gbcol[:, dcol:dcol + 1], None, op0=ALU.mult),
                [B("e2_t"), B("gbcol")], [B("e2_t")], "f2_gb")
        for ssub in range(4):
            t_i = w * 4 + ssub
            for half in range(2):
                for dd in range(4):
                    top(lambda t, ssub=ssub, half=half, dd=dd: t.transpose(
                            p7[:, bass.ts(dd, 128)],
                            e2_t[:, half * 4 + dd, bass.ts(ssub, 128)], identf[:, :]),
                        [B("e2_t"), B("identf")], [B("p7")], "f2_T")
                dma(lambda e, t_i=t_i, half=half: e.dma_start(
                        out=xr_t[:, bass.ts(half, 512)],
                        in_=xres_b.ap()[bass.ts(t_i, 128), bass.ts(half, 512)]),
                    [B("xr_t"), B("xres_bd")], [B("xr_t")], "pc_xld")
                vop(lambda v, half=half: v.tensor_tensor(
                        out_t[:, bass.ts(half, 512)], xr_t[:, bass.ts(half, 512)], p7[:, :],
                        ALU.add),
                    [B("xr_t"), B("p7")], [B("out_t")], "f2_add")
            dma(lambda e, t_i=t_i: e.dma_start(out=out_own.ap()[bass.ts(t_i, 128), :],
                                               in_=out_t[:, :]),
                [B("out_t")], [B("out_d")], "out_st")

    fw.final_dma_barrier("sync")
    fw.finalize(ex)
    ffnB.close()
    ffn.close()
    ex.close()
    return nc


_NC_CACHE = None


def kernel(L0_attn_x, L0_attn_W_in, L0_attn_conv_w, L0_attn_conv_b, L0_attn_W_x,
           L0_attn_W_dt, L0_attn_b_dt, L0_attn_A_log, L0_attn_D, L0_attn_W_out,
           L0_ffn_W1, L0_ffn_b1, L0_ffn_W2, L0_ffn_b2, L0_c, L0_wmod):
    global _NC_CACHE
    if _NC_CACHE is None:
        _NC_CACHE = build_nc()
    nc = _NC_CACHE

    f = np.float32
    X = np.ascontiguousarray(np.asarray(L0_attn_x, dtype=f))
    W_in = np.asarray(L0_attn_W_in, f)
    conv_w = np.asarray(L0_attn_conv_w, f)
    conv_b = np.asarray(L0_attn_conv_b, f)
    W_x = np.asarray(L0_attn_W_x, f)
    W_dt = np.asarray(L0_attn_W_dt, f)
    b_dt = np.asarray(L0_attn_b_dt, f)
    A_log = np.asarray(L0_attn_A_log, f)
    Dv = np.asarray(L0_attn_D, f)
    W_out = np.asarray(L0_attn_W_out, f)
    W1 = np.asarray(L0_ffn_W1, f)
    b1 = np.asarray(L0_ffn_b1, f)
    W2 = np.asarray(L0_ffn_W2, f)
    b2 = np.asarray(L0_ffn_b2, f)
    Cc = np.asarray(L0_c, f)
    Wmod = np.asarray(L0_wmod, f)

    def cols(v):
        return np.ascontiguousarray(v.reshape(NCH, 128).T)

    w1t = np.ascontiguousarray(W1.T)
    b1c = np.ascontiguousarray(b1.reshape(NHC, 128).T)
    w2t = np.ascontiguousarray(W2.T)
    b2c = np.ascontiguousarray(b2.reshape(NDC, 128).T)
    ident = np.eye(128, dtype=f)
    ones_r = np.ones((1, 128), f)
    tri_c = np.ascontiguousarray(np.triu(np.ones((128, 128), f)))
    const_c = np.ascontiguousarray(
        np.tile(np.array([[1e-5, 1.0, RND, 0.6]], f), (128, 1)))

    in_maps = []
    for c in range(8):
        b_i, j = c // 2, c % 2
        ch = slice(j * DH, (j + 1) * DH)
        in_maps.append({
            "x_own": np.ascontiguousarray(X[b_i, j * SH:(j + 1) * SH, :]),
            "c_own": np.ascontiguousarray(Cc[b_i:b_i + 1, :]),
            "wmod": Wmod,
            "w_in_own": np.ascontiguousarray(
                np.concatenate([W_in[:, j * DH:(j + 1) * DH],
                                W_in[:, DI + j * DH:DI + (j + 1) * DH]], axis=1)),
            "conv_w_c": np.ascontiguousarray(
                conv_w[ch].reshape(NCH, 128, 4).transpose(1, 0, 2)),
            "conv_b_c": cols(conv_b[ch]),
            "w_x_own": np.ascontiguousarray(W_x[ch]),
            "w_dt_own": np.ascontiguousarray(W_dt[:, ch]),
            "b_dt_c": cols(b_dt[ch]),
            "a_log_c": np.ascontiguousarray(
                A_log[ch].reshape(NCH, 128, KS).transpose(1, 0, 2)),
            "d_c": cols(Dv[ch]),
            "w_out_own": np.ascontiguousarray(W_out[ch]),
            "w1t": w1t,
            "b1_c": b1c,
            "w2t": w2t,
            "b2_c": b2c,
            "ident_in": ident,
            "ones_r": ones_r,
            "tri_in": tri_c,
            "const_in": const_c,
        })

    res = run_bass_kernel_spmd(nc, in_maps, core_ids=list(range(8)))
    out = np.empty((4, S, D), f)
    for c in range(8):
        b_i, j = c // 2, c % 2
        out[b_i, j * SH:(j + 1) * SH, :] = np.asarray(res.results[c]["out_own"], f)
    return out


# version 22
# speedup vs baseline: 1.0685x; 1.0685x over previous
import contextlib
import numpy as np

import concourse.bass as bass
import concourse.mybir as mybir
from concourse.bass_utils import run_bass_kernel_spmd

F32 = mybir.dt.float32
BF16 = mybir.dt.bfloat16
I32 = mybir.dt.int32
AF = mybir.ActivationFunctionType
ALU = mybir.AluOpType
AX = mybir.AxisListType

ENGINES = ("tensor", "vector", "scalar", "gpsimd", "sync")

D = 1024
DI = 2048
DH = 1024
NCH = 8
S = 4096
SH = 2048
STR = 256
NSTR = 16
NW = 8
FW = 4
KS = 16
RNK = 64
H = 4096
NHC = 32
NDC = 8
PAIRS = [[0, 1], [2, 3], [4, 5], [6, 7]]
ALL8 = [list(range(8))]
RND = 256.5


class Buf:
    __slots__ = ("name", "w", "r")

    def __init__(self, name):
        self.name = name
        self.w = None
        self.r = []


class _Op:
    __slots__ = ("eng", "fn", "waits", "incs", "tag")

    def __init__(self, eng, fn, waits, incs, tag):
        self.eng, self.fn, self.waits, self.incs, self.tag = eng, fn, waits, incs, tag


class Fw:

    def __init__(self, nc, dma_slots=8):
        self.nc = nc
        self.ops = {e: [] for e in ENGINES}
        self.sem_count = {}
        self.hw = {}
        self.sem_handles = {}
        self.sem_names = []
        self.dma_slots = dma_slots
        self._dma_rr = {}

    def _sem(self, name):
        if name not in self.sem_count:
            self.sem_count[name] = 0
            self.sem_names.append(name)
        return name

    def buf(self, name):
        return Buf(name)

    def _collect_waits(self, eng, reads, writes):
        need = {}

        def add(tok):
            if tok is None:
                return
            s, v = tok
            if need.get(s, -1) < v:
                need[s] = v

        for b in reads:
            add(b.w)
        for b in writes:
            add(b.w)
            for t in b.r:
                add(t)
        waits = []
        for s, v in need.items():
            k = (eng, s)
            if self.hw.get(k, 0) >= v:
                continue
            self.hw[k] = v
            waits.append((s, v))
        return waits

    def op(self, eng, fn, reads=(), writes=(), tag=""):
        waits = self._collect_waits(eng, reads, writes)
        sem = self._sem("prog_" + eng)
        self.sem_count[sem] += 1
        tok = (sem, self.sem_count[sem])
        for b in writes:
            b.w = tok
            b.r = []
        for b in reads:
            b.r.append(tok)
        self.ops[eng].append(_Op(eng, fn, waits, [(sem, 1)], tag))
        return tok

    def dma(self, eng, fn, reads=(), writes=(), tag=""):
        rr = self._dma_rr.get(eng, 0)
        self._dma_rr[eng] = rr + 1
        sem = self._sem(f"dma_{eng}{rr % self.dma_slots}")
        waits = self._collect_waits(eng, reads, writes)
        prev = self.sem_count[sem]
        if prev:
            k = (eng, sem)
            if self.hw.get(k, 0) < prev:
                self.hw[k] = prev
                waits = waits + [(sem, prev)]
        self.sem_count[sem] += 16
        tok = (sem, self.sem_count[sem])
        for b in writes:
            b.w = tok
            b.r = []
        for b in reads:
            b.r.append(tok)
        self.ops[eng].append(_Op(eng, fn, waits, [(sem, 16)], tag))
        return tok

    def collective(self, fn, reads=(), writes=(), tag="cc"):
        waits = self._collect_waits("gpsimd", reads, writes)
        sem = self._sem("cc")
        self.sem_count[sem] += 1
        tok = (sem, self.sem_count[sem])
        for b in writes:
            b.w = tok
            b.r = []
        for b in reads:
            b.r.append(tok)
        self.ops["gpsimd"].append(_Op("gpsimd", fn, waits, [(sem, 1)], tag))
        return tok

    def phase_barrier(self):
        snapshot = [(s, v) for s, v in self.sem_count.items() if v > 0]
        for e in ENGINES:
            waits = []
            for s, v in snapshot:
                k = (e, s)
                if self.hw.get(k, 0) >= v:
                    continue
                self.hw[k] = v
                waits.append((s, v))
            if waits:
                self.ops[e].append(_Op(e, None, waits, [], "phase_barrier"))

    def barrier_wait(self, eng, tok, tag="barrier"):
        if tok is None:
            return
        s, v = tok
        k = (eng, s)
        if self.hw.get(k, 0) >= v:
            return
        self.hw[k] = v
        self.ops[eng].append(_Op(eng, None, [(s, v)], [], tag))

    def final_dma_barrier(self, eng):
        for s, v in self.sem_count.items():
            if not (s.startswith("dma_") or s == "cc"):
                continue
            k = (eng, s)
            if self.hw.get(k, 0) >= v:
                continue
            self.hw[k] = v
            self.ops[eng].append(_Op(eng, None, [(s, v)], [], "final"))

    def finalize(self, exit_ctx):
        nc = self.nc
        for name in self.sem_names:
            self.sem_handles[name] = exit_ctx.enter_context(nc.semaphore(name))
        block = exit_ctx.enter_context(nc.Block())
        for eng_name in ENGINES:
            ops = self.ops[eng_name]
            if not ops:
                continue

            def make(eng_ops):
                def emit(eng):
                    for o in eng_ops:
                        for s, v in o.waits:
                            eng.wait_ge(self.sem_handles[s], v)
                        if o.fn is None:
                            continue
                        inst = o.fn(eng)
                        assert inst is not None, f"op {o.tag} returned None"
                        for s, n in o.incs:
                            inst.then_inc(self.sem_handles[s], n)
                return emit

            getattr(block, eng_name)(make(ops))


def build_nc():
    nc = bass.Bass("TRN2", target_bir_lowering=False)
    fw = Fw(nc)

    def IN(name, shape, dtype=F32):
        return nc.dram_tensor(name, shape, dtype, kind="ExternalInput")

    x_own = IN("x_own", [SH, D])
    c_own = IN("c_own", [1, D])
    wmod = IN("wmod", [D, 6 * D])
    w_in_own = IN("w_in_own", [D, 2 * DH])
    conv_w_c = IN("conv_w_c", [128, NCH, 4])
    conv_b_c = IN("conv_b_c", [128, NCH])
    w_x_own = IN("w_x_own", [DH, 96])
    w_dt_own = IN("w_dt_own", [RNK, DH])
    b_dt_c = IN("b_dt_c", [128, NCH])
    b_dt_row = IN("b_dt_row", [1, DH])
    a_log_c = IN("a_log_c", [128, NCH, KS])
    d_c = IN("d_c", [128, NCH])
    w_out_own = IN("w_out_own", [DH, D])
    w1t = IN("w1t", [D, H])
    b1_c = IN("b1_c", [128, NHC])
    w2t = IN("w2t", [H, D])
    b2_c = IN("b2_c", [128, NDC])
    ident_in = IN("ident_in", [128, 128])
    ones_r = IN("ones_r", [1, 128])
    tri_in = IN("tri_in", [128, 128])
    const_in = IN("const_in", [128, 4])

    out_own = nc.dram_tensor("out_own", [SH, D], F32, kind="ExternalOutput")

    hn_in = nc.dram_tensor("hn_in", [D, SH], BF16)
    hn_out = nc.dram_tensor("hn_out", [2, D, SH], BF16)
    ssmp_in = [nc.dram_tensor(f"ssmp_in{i}", [96, STR], F32) for i in range(NSTR)]
    ssmp_out = [nc.dram_tensor(f"ssmp_out{i}", [96, STR], F32) for i in range(NSTR)]
    att_in = [nc.dram_tensor(f"att_in{i}", [2, STR, D], BF16) for i in range(NW)]
    att_out = [nc.dram_tensor(f"att_out{i}", [STR, D], BF16) for i in range(NW)]
    mmx_in = nc.dram_tensor("mmx_in", [1, 2], F32)
    mmx_out = nc.dram_tensor("mmx_out", [1, 2], F32)
    gmx_in = nc.dram_tensor("gmx_in", [1, 2], F32)
    gmx_out = nc.dram_tensor("gmx_out", [1, 2], F32)
    xres_b = nc.dram_tensor("xres_b", [SH, D], F32)
    g_b = nc.dram_tensor("g_b", [H, SH], F32)
    wq1_b = nc.dram_tensor("wq1_b", [D, H], BF16)

    ex = contextlib.ExitStack()

    def sb(name, shape, dtype=F32, stack=None):
        return (stack or ex).enter_context(nc.sbuf_tensor(name, shape, dtype))

    def psum(name, shape, dtype=F32):
        return ex.enter_context(nc.psum_tensor(name, shape, dtype))

    pHh = [psum("pHa", [128, 512]), psum("pHb", [128, 512])]
    yTh = [psum("yTa", [128, 512]), psum("yTb", [128, 512])]
    p0 = psum("p0", [128, 512])
    p1 = psum("p1", [128, 512])
    p6b = psum("p6b", [128, 1024], BF16)
    p7 = psum("p7", [128, 512])

    bufs = {}

    def B(name):
        if name not in bufs:
            bufs[name] = fw.buf(name)
        return bufs[name]

    def vop(fn, reads, writes, tag=""):
        return fw.op("vector", fn, reads, writes, tag)

    def sop(fn, reads, writes, tag=""):
        return fw.op("scalar", fn, reads, writes, tag)

    def top(fn, reads, writes, tag=""):
        return fw.op("tensor", fn, reads, writes, tag)

    def dma(fn, reads, writes, tag="", eng="sync"):
        return fw.dma(eng, fn, reads, writes, tag)

    identf = sb("identf", [128, 128])
    identb = sb("identb", [128, 128], BF16)
    onesrf = sb("onesrf", [1, 128])
    onesrb = sb("onesrb", [1, 128], BF16)
    onescb = sb("onescb", [128, 1], BF16)
    trif = sb("trif", [128, 128])
    trib = sb("trib", [128, 128], BF16)
    akb = sb("akb", [128, KS])
    nakb = sb("nakb", [128, KS])
    constc = sb("constc", [128, 4])

    w_in_s = sb("w_in_s", [128, NCH, 2 * DH], BF16)
    w_out_s = sb("w_out_s", [128, NCH, D], BF16)
    w_x_s = sb("w_x_s", [128, NCH, 96], BF16)
    w_dt_s = sb("w_dt_s", [RNK, DH], BF16)
    a_cols = sb("a_cols", [128, NCH, KS])
    d_cols = sb("d_cols", [128, NCH])
    cw_cols = sb("cw_cols", [128, NCH, 4])
    cb_cols = sb("cb_cols", [128, NCH])
    bdt_cols = sb("bdt_cols", [128, NCH])
    sa_cols = sb("sa_cols", [128, 16])
    mod48 = sb("mod48", [48, 128])
    modT = sb("modT", [128, 48])
    gacol = sb("gacol", [128, NCH])
    gbcol = sb("gbcol", [128, NDC])
    scal = sb("scal", [1, 32])
    scalb = sb("scalb", [128, 16])

    dma(lambda e: e.dma_start(out=identf[:, :], in_=ident_in.ap()), [], [B("identf")])
    dma(lambda e: e.dma_start(out=constc[:, :], in_=const_in.ap()), [], [B("constc")])
    dma(lambda e: e.dma_start(out=trif[:, :], in_=tri_in.ap()), [], [B("trif")])
    vop(lambda v: v.tensor_copy(trib[:, :], trif[:, :]), [B("trif")], [B("trib")])
    nc.const_aps.register(F32, 1e-5, constc[:, 0:1])
    nc.const_aps.register(F32, 1.0, constc[:, 1:2])
    nc.const_aps.register(F32, RND, constc[:, 2:3])
    dma(lambda e: e.dma_start(out=onesrf[:, :], in_=ones_r.ap()), [], [B("onesrf")])
    vop(lambda v: v.tensor_copy(identb[:, :], identf[:, :]), [B("identf")], [B("identb")])
    vop(lambda v: v.tensor_copy(onesrb[:, :], onesrf[:, :]), [B("onesrf")], [B("onesrb")])
    top(lambda t: t.transpose(p6b[:, 0:1], onesrb[:, :], identb[0:1, 0:1]),
        [B("onesrb"), B("identb")], [B("p6b")])
    sop(lambda s: s.copy(out=onescb[:, :], in_=p6b[:, 0:1]), [B("p6b")], [B("onescb")])

    dma(lambda e: e.dma_start(out=cw_cols.ap(), in_=conv_w_c.ap()), [], [B("cw")])
    dma(lambda e: e.dma_start(out=cb_cols.ap(), in_=conv_b_c.ap()), [], [B("cb")])
    dma(lambda e: e.dma_start(out=bdt_cols.ap(), in_=b_dt_c.ap()), [], [B("bdt")])
    dma(lambda e: e.dma_start(out=d_cols.ap(), in_=d_c.ap()), [], [B("dcol")])
    dma(lambda e: e.dma_start(out=a_cols.ap(), in_=a_log_c.ap()), [], [B("acol")])
    sop(lambda s: s.activation(a_cols[:, :, :], a_cols[:, :, :], AF.Exp),
        [B("acol")], [B("acol")])
    vop(lambda v: v.tensor_scalar(a_cols[:, :, :], a_cols[:, :, :], -1.0, None, op0=ALU.mult),
        [B("acol")], [B("acol")])
    for k in range(KS):
        top(lambda t, k=k: t.matmul(p7[:, 0:1], onesrf[:, :], a_cols[0:1, 0, k:k + 1],
                                    start=True, stop=True),
            [B("acol"), B("onesrf"), B("p7")], [B("p7")], "akbc")
        sop(lambda s, k=k: s.copy(out=akb[:, k:k + 1], in_=p7[:, 0:1]),
            [B("p7")], [B("akb")])
    vop(lambda v: v.tensor_scalar(nakb[:, :], akb[:, :], -1.0, None, op0=ALU.mult),
        [B("akb")], [B("nakb")])

    with contextlib.ExitStack() as wx_ex:
        wx_f = sb("wx_f", [128, NCH, 96], F32, wx_ex)
        wdt_f = sb("wdt_f", [RNK, DH], F32, wx_ex)
        dma(lambda e: e.dma_start(out=wx_f.ap(),
                                  in_=w_x_own.ap().rearrange("(c p) n -> p c n", p=128)),
            [], [B("wxf")])
        vop(lambda v: v.tensor_copy(w_x_s[:, :, :], wx_f[:, :, :]), [B("wxf")], [B("wxs")])
        dma(lambda e: e.dma_start(out=wdt_f.ap(), in_=w_dt_own.ap()), [], [B("wdtf")])
        vop(lambda v: v.tensor_copy(w_dt_s[:, :], wdt_f[:, :]), [B("wdtf")], [B("wdts")])

        cc = sb("cc", [1, D], F32, wx_ex)
        ccT = sb("ccT", [128, NDC], F32, wx_ex)
        ccTb = sb("ccTb", [128, NDC], BF16, wx_ex)
        wm_f2 = [sb(f"wm_f{i}", [128, 512], F32, wx_ex) for i in range(2)]
        wm_b2 = [sb(f"wm_b{i}", [128, 512], BF16, wx_ex) for i in range(2)]
        mrow = sb("mrow", [1, 512], F32, wx_ex)
        dma(lambda e: e.dma_start(out=cc.ap(), in_=c_own.ap()), [], [B("cc")])
        sop(lambda s: s.activation(cc[:, :], cc[:, :], AF.Silu), [B("cc")], [B("cc")])
        for dc in range(NDC):
            top(lambda t, dc=dc: t.transpose(p7[:, 0:1], cc[0:1, bass.ts(dc, 128)], identf[0:1, 0:1]),
                [B("cc"), B("identf")], [B("p7")], "ccT")
            sop(lambda s, dc=dc: s.copy(out=ccT[:, dc:dc + 1], in_=p7[:, 0:1]),
                [B("p7")], [B("ccT")])
        vop(lambda v: v.tensor_copy(ccTb[:, :], ccT[:, :]), [B("ccT")], [B("ccTb")])
        for g in range(12):
            for dc in range(NDC):
                bi = dc % 2
                wm_f, wm_b = wm_f2[bi], wm_b2[bi]
                dma(lambda e, g=g, dc=dc, wm_f=wm_f: e.dma_start(
                        out=wm_f[:, :], in_=wmod.ap()[bass.ts(dc, 128), bass.ts(g, 512)]),
                    [B(f"wmf{bi}")], [B(f"wmf{bi}")], "wmod_ld")
                vop(lambda v, wm_f=wm_f, wm_b=wm_b: v.tensor_copy(wm_b[:, :], wm_f[:, :]),
                    [B(f"wmf{bi}")], [B(f"wmb{bi}")])
                top(lambda t, dc=dc, wm_b=wm_b: t.matmul(p7[0:1, :], ccTb[:, dc:dc + 1], wm_b[:, :],
                                              start=(dc == 0), stop=(dc == NDC - 1)),
                    [B("ccTb"), B(f"wmb{bi}"), B("p7")], [B("p7")], "modmm")
            sop(lambda s: s.copy(out=mrow[:, :], in_=p7[0:1, :]), [B("p7")], [B("mrow")])
            dma(lambda e, g=g: e.dma_start(
                    out=mod48.ap()[g * 4:(g + 1) * 4, :], in_=mrow.ap()),
                [B("mrow")], [B("mod48")], "mod48")
        top(lambda t: t.transpose(p7[:, 0:48], mod48[:, :], identf[0:48, 0:48]),
            [B("mod48"), B("identf")], [B("p7")])
        sop(lambda s: s.copy(out=modT[:, :], in_=p7[:, 0:48]), [B("p7")], [B("modT")])
        sgn = sb("sgn", [128, NCH], F32, wx_ex)
        mag = sb("mag", [128, NCH], F32, wx_ex)
        for (dst, base) in ((gacol, 16), (gbcol, 40)):
            vop(lambda v, base=base: v.tensor_scalar(sgn[:, :], modT[:, base:base + 8], 0.0, None, op0=ALU.is_ge),
                [B("modT")], [B("sgn")])
            vop(lambda v: v.tensor_scalar(sgn[:, :], sgn[:, :], 2.0, -1.0, op0=ALU.mult, op1=ALU.add),
                [B("sgn")], [B("sgn")])
            sop(lambda s, base=base: s.activation(mag[:, :], modT[:, base:base + 8], AF.Abs),
                [B("modT")], [B("mag")])
            vop(lambda v: v.tensor_scalar(mag[:, :], mag[:, :], 0.25, None, op0=ALU.max),
                [B("mag")], [B("mag")])
            vop(lambda v, dst=dst: v.tensor_tensor(dst[:, :], sgn[:, :], mag[:, :], ALU.mult),
                [B("sgn"), B("mag")], [B(dst.name)])

        capac = sb("capac", [128, NCH], F32, wx_ex)
        saTb = sb("saTb", [128, NCH], BF16, wx_ex)
        vop(lambda v: v.tensor_scalar(capac[:, :], modT[:, 8:16], 1.0, None, op0=ALU.add),
            [B("modT")], [B("capac")])
        vop(lambda v: v.tensor_copy(saTb[:, :], modT[:, 0:8]), [B("modT")], [B("saTb")])
        wi_f2 = [sb(f"wi_f{i}", [128, 512], F32, wx_ex) for i in range(2)]
        wi_b2 = [sb(f"wi_b{i}", [128, 512], BF16, wx_ex) for i in range(2)]
        sarow = sb("sarow", [1, 512], F32, wx_ex)
        for g in range(4):
            for dc in range(NDC):
                bi = dc % 2
                wi_f, wi_b = wi_f2[bi], wi_b2[bi]
                dma(lambda e, g=g, dc=dc, wi_f=wi_f: e.dma_start(
                        out=wi_f[:, :], in_=w_in_own.ap()[bass.ts(dc, 128), bass.ts(g, 512)]),
                    [B(f"wif{bi}")], [B(f"wif{bi}")], "win_ld")
                vop(lambda v, wi_f=wi_f, wi_b=wi_b: v.tensor_copy(wi_b[:, :], wi_f[:, :]),
                    [B(f"wif{bi}")], [B(f"wib{bi}")])
                top(lambda t, dc=dc, wi_b=wi_b: t.matmul(p7[0:1, :], saTb[:, dc:dc + 1], wi_b[:, :],
                                              start=(dc == 0), stop=(dc == NDC - 1)),
                    [B("saTb"), B(f"wib{bi}"), B("p7")], [B("p7")], "sarow")
                vop(lambda v, g=g, dc=dc, wi_f=wi_f: v.tensor_scalar(
                        w_in_s[:, dc, bass.ts(g, 512)], wi_f[:, :], capac[:, dc:dc + 1], None, op0=ALU.mult),
                    [B(f"wif{bi}"), B("capac")], [B("wins")], "winfold")
            sop(lambda s: s.copy(out=sarow[:, :], in_=p7[0:1, :]), [B("p7")], [B("sarow")])
            for q in range(4):
                top(lambda t, q=q: t.transpose(p7[:, 0:1], sarow[0:1, bass.ts(q, 128)], identf[0:1, 0:1]),
                    [B("sarow"), B("identf"), B("p7")], [B("p7")], "saT")
                sop(lambda s, g=g, q=q: s.copy(out=sa_cols[:, 4 * g + q:4 * g + q + 1], in_=p7[:, 0:1]),
                    [B("p7")], [B("sacols")])

        wo_f2 = [sb(f"wo_f{i}", [128, 512], F32, wx_ex) for i in range(2)]
        for c in range(NCH):
            for g in range(2):
                bi = (c * 2 + g) % 2
                wo_f = wo_f2[bi]
                dma(lambda e, c=c, g=g, wo_f=wo_f: e.dma_start(
                        out=wo_f[:, :], in_=w_out_own.ap()[bass.ts(c, 128), bass.ts(g, 512)]),
                    [B(f"wof{bi}")], [B(f"wof{bi}")], "wout_ld")
                vop(lambda v, c=c, g=g, wo_f=wo_f: v.tensor_copy(w_out_s[:, c, bass.ts(g, 512)], wo_f[:, :]),
                    [B(f"wof{bi}")], [B("wouts")])

    mam = contextlib.ExitStack()
    xtile2 = [sb(f"xtile{i}", [128, D], F32, mam) for i in range(2)]
    sqt = sb("sqt", [128, D], F32, mam)
    stats2 = [sb(f"stats{i}", [128, 8], F32, mam) for i in range(2)]
    hn_tile2 = [sb(f"hn_tile{i}", [128, D], BF16, mam) for i in range(2)]
    hnT_t = sb("hnT_t", [128, NDC, 128], BF16, mam)

    for t_i in range(SH // 128):
        bi = t_i % 2
        xtile, stats, hn_tile = xtile2[bi], stats2[bi], hn_tile2[bi]
        xb_, stb, hnb_ = B(f"xtile{bi}"), B(f"stats{bi}"), B(f"hn_tile{bi}")
        dma(lambda e, t_i=t_i, xtile=xtile: e.dma_start(out=xtile[:, :], in_=x_own.ap()[bass.ts(t_i, 128), :]),
            [xb_], [xb_], "lna_x")
        vop(lambda v, xtile=xtile, stats=stats: v.tensor_reduce(stats[:, 0:1], xtile[:, :], AX.X, ALU.add),
            [xb_], [stb])
        sop(lambda s, xtile=xtile, stats=stats: s.activation(sqt[:, :], xtile[:, :], AF.Square,
                                                             accum_out=stats[:, 1:2]),
            [xb_, B("sqt")], [B("sqt"), stb])
        vop(lambda v, stats=stats: v.tensor_scalar(stats[:, 2:3], stats[:, 0:1], 1.0 / D, None, op0=ALU.mult),
            [stb], [stb])
        sop(lambda s, stats=stats: s.activation(stats[:, 3:4], stats[:, 2:3], AF.Square),
            [stb], [stb])
        vop(lambda v, stats=stats: v.tensor_scalar(stats[:, 4:5], stats[:, 1:2], 1.0 / D, stats[:, 3:4],
                                      op0=ALU.mult, op1=ALU.subtract),
            [stb], [stb])
        sop(lambda s, stats=stats: s.activation(stats[:, 5:6], stats[:, 4:5], AF.Sqrt, bias=1e-5),
            [stb], [stb])
        vop(lambda v, stats=stats: v.reciprocal(stats[:, 6:7], stats[:, 5:6]), [stb], [stb])
        vop(lambda v, xtile=xtile, stats=stats, hn_tile=hn_tile: v.tensor_scalar(
                hn_tile[:, :], xtile[:, :], stats[:, 2:3], stats[:, 6:7],
                op0=ALU.subtract, op1=ALU.mult),
            [xb_, stb], [hnb_])
        for dc in range(NDC):
            top(lambda t, dc=dc, hn_tile=hn_tile: t.transpose(p6b[:, bass.ts(dc, 128)],
                                                              hn_tile[:, bass.ts(dc, 128)], identb[:, :]),
                [hnb_, B("identb"), B("p6b")], [B("p6b")], "lnaT")
        sop(lambda s: s.copy(out=hnT_t[:, :, :], in_=p6b.ap().rearrange("p (c n) -> p c n", c=NDC)),
            [B("p6b")], [B("hnT")], "lnaTe")
        dma(lambda e, t_i=t_i: e.dma_start(
                out=hn_in.ap().rearrange("(c p) s -> p c s", p=128)[:, :, bass.ts(t_i, 128)],
                in_=hnT_t.ap()),
            [B("hnT")], [B("hn_in_d")], "lna_st")

    tok_hn = fw.collective(
        lambda g: g.collective_compute("AllGather", ALU.bypass, replica_groups=PAIRS,
                                       ins=[hn_in.ap().opt()], outs=[hn_out.ap().opt()]),
        reads=[B("hn_in_d")], writes=[B("hn_out_d")], tag="hn_ag")

    hn_str2 = [sb(f"hn_str{i}", [128, NDC, STR], BF16, mam) for i in range(2)]
    xin_str = sb("xin_str", [128, NCH, STR + 3], F32, mam)
    xact2 = [sb(f"xact_str{i}", [128, NCH, STR], F32, mam) for i in range(2)]
    xact_b = sb("xact_b", [128, NCH, STR], BF16, mam)
    sz_str2 = [sb(f"sz_str{i}", [128, NCH, STR], BF16, mam) for i in range(2)]
    q_str = sb("q_str", [128, NCH, STR], BF16, mam)
    ssmp_t = sb("ssmp_t", [96, STR], F32, mam)
    dtr_b = sb("dtr_b", [RNK, STR], BF16, mam)
    cv2b = [sb(f"cv{i}", [128, STR], F32, mam) for i in range(2)]
    wv = [sb(f"wv{i}", [128, 1024], BF16, mam) for i in range(4)]
    wt = [sb(f"wt{i}", [128, 1024], BF16, mam) for i in range(4)]
    crow = sb("crow", [KS, 1024], BF16, mam)
    egp = [sb(f"eg{i}", [128, 1024], F32, mam) for i in range(4)]
    dtTM = sb("dtTM", [128, 1024], F32, mam)
    ySBb = sb("ySBb", [128, 1024], BF16, mam)
    uTM = [sb(f"uTM{i}", [128, 1024], F32, mam) for i in range(2)]
    phs = [sb(f"phs{i}", [128, 1024], F32, mam) for i in range(2)]
    BCt2 = [sb(f"BCt{i}", [128, 2, 32], F32, mam) for i in range(2)]
    diag = [sb(f"diag{i}", [128, 128], BF16, mam) for i in range(2)]
    qt_t2 = [sb(f"qt_t{i}", [128, STR], F32, mam) for i in range(2)]
    xdb2 = [sb(f"xdb{i}", [128, NCH, STR], BF16, mam) for i in range(2)]
    attf2 = [sb(f"attf{i}", [128, STR], BF16, mam) for i in range(2)]
    attnTM = sb("attnTM", [128, 2, D], BF16, mam)
    rs_t = sb("rs_t", [128, 1, D], BF16, mam)
    bdtrow = sb("bdtrow", [1, DH], F32, mam)
    bdtbc = sb("bdtbc", [128, DH], F32, mam)
    xown_t = None

    vop(lambda v: v.memset(xin_str[:, :, 0:3], 0.0), [], [B(f"xin{c}") for c in range(NCH)])
    dma(lambda e: e.dma_start(out=bdtrow[:, :], in_=b_dt_row.ap()), [], [B("bdtrow")])
    for _h in range(2):
        top(lambda t, _h=_h: t.matmul(p7[:, :], onesrf[:, :], bdtrow[0:1, bass.ts(_h, 512)],
                                      start=True, stop=True),
            [B("onesrf"), B("bdtrow"), B("p7")], [B("p7")], "bdtbc_mm")
        sop(lambda s, _h=_h: s.copy(out=bdtbc[:, bass.ts(_h, 512)], in_=p7[:, :]),
            [B("p7")], [B("bdtbc")], "bdtbc_cp")

    def gen_A1(st):
        if st >= NSTR:
            return []
        o = []
        p = st % 2
        hn_str = hn_str2[p]
        sz_str = sz_str2[p]
        hnb = B(f"hn_str{p}")
        szb = B(f"sz_str{p}")
        xact_str = xact2[p]
        xdb = xdb2[p]
        blk = st // 8
        scol = (st % 8) * STR
        o.append(lambda: dma(lambda e, blk=blk, scol=scol, hn_str=hn_str: e.dma_start(
                out=hn_str.ap(),
                in_=hn_out.ap().rearrange("b (c p) s -> p b c s", p=128)[:, blk, :, scol:scol + STR]),
            [hnb, B("hn_out_d")], [hnb], "hn_ld"))
        for c in range(NCH):
            pb, pbuf = (p0, B("p0")) if c % 2 == 0 else (p1, B("p1"))
            o.append(lambda c=c, pb=pb, pbuf=pbuf, hn_str=hn_str, hnb=hnb: top(
                lambda t, c=c, pb=pb, hn_str=hn_str: [
                    t.matmul(pb[:, 0:STR], w_in_s[:, dc, bass.ts(c, 128)], hn_str[:, dc, :],
                             start=(dc == 0), stop=(dc == NDC - 1))
                    for dc in range(NDC)][-1],
                [B("wins"), hnb, pbuf], [pbuf], "win_mm_x"))
            o.append(lambda c=c, pb=pb, pbuf=pbuf: vop(
                lambda v, c=c, pb=pb: v.tensor_scalar(xin_str[:, c, 3:STR + 3], pb[:, 0:STR],
                                                      sa_cols[:, c:c + 1], None, op0=ALU.add),
                [pbuf, B("sacols")], [B(f"xin{c}")], "xin_ev"))
            o.append(lambda c=c, p=p, xdb=xdb: vop(
                lambda v, c=c, xdb=xdb: v.tensor_scalar(xdb[:, c, :], xin_str[:, c, 3:STR + 3],
                                                        d_cols[:, c:c + 1], None, op0=ALU.mult),
                [B(f"xin{c}"), B("dcol")], [B(f"xdb{c}_{p}")], "xd"))
        for c in range(NCH):
            pb, pbuf = (p0, B("p0")) if c % 2 == 0 else (p1, B("p1"))
            o.append(lambda c=c, pb=pb, pbuf=pbuf, hn_str=hn_str, hnb=hnb: top(
                lambda t, c=c, pb=pb, hn_str=hn_str: [
                    t.matmul(pb[:, 0:STR], w_in_s[:, dc, DH + 128 * c:DH + 128 * (c + 1)], hn_str[:, dc, :],
                             start=(dc == 0), stop=(dc == NDC - 1))
                    for dc in range(NDC)][-1],
                [B("wins"), hnb, pbuf], [pbuf], "win_mm_z"))
            o.append(lambda c=c, pb=pb, pbuf=pbuf, sz_str=sz_str, szb=szb: sop(
                lambda s, c=c, pb=pb, sz_str=sz_str: s.activation(sz_str[:, c, :], pb[:, 0:STR],
                                                                  AF.Silu, bias=sa_cols[:, 8 + c:9 + c]),
                [pbuf, B("sacols")], [szb], "sz_ev"))
        for c in range(NCH):
            cv = cv2b[c % 2]
            cvb = B(f"cv{c % 2}")
            o.append(lambda c=c, cv=cv, cvb=cvb: vop(
                lambda v, c=c, cv=cv: v.tensor_scalar(cv[:, :], xin_str[:, c, 0:STR], cw_cols[:, c, 0:1],
                                                      None, op0=ALU.mult),
                [B(f"xin{c}"), B("cw"), cvb], [cvb], "conv0"))
            for j in range(1, 4):
                o.append(lambda c=c, j=j, cv=cv, cvb=cvb: vop(
                    lambda v, c=c, j=j, cv=cv: v.scalar_tensor_tensor(
                        cv[:, :], xin_str[:, c, j:STR + j], cw_cols[:, c, j:j + 1], cv[:, :],
                        ALU.mult, ALU.add),
                    [B(f"xin{c}"), B("cw"), cvb], [cvb], "conv"))
            o.append(lambda c=c, p=p, cv=cv, cvb=cvb, xact_str=xact_str: sop(
                lambda s, c=c, cv=cv, xact_str=xact_str: s.activation(
                    xact_str[:, c, :], cv[:, :], AF.Silu, bias=cb_cols[:, c:c + 1]),
                [cvb, B("cb")], [B(f"xact{c}_{p}")], "xact"))
            o.append(lambda c=c, p=p, xact_str=xact_str: vop(
                lambda v, c=c, xact_str=xact_str: v.tensor_copy(xact_b[:, c, :], xact_str[:, c, :]),
                [B(f"xact{c}_{p}")], [B(f"xactb{c}")], "xact_b"))
        o.append(lambda: top(lambda t: [t.matmul(p0[0:96, 0:STR], w_x_s[:, c, :], xact_b[:, c, :],
                                start=(c == 0), stop=(c == NCH - 1)) for c in range(NCH)][-1],
            [B("wxs"), B("p0")] + [B(f"xactb{c}") for c in range(NCH)], [B("p0")], "ssmp_mm"))
        o.append(lambda: sop(lambda s: s.copy(out=ssmp_t[:, :], in_=p0[0:96, 0:STR]),
                             [B("p0")], [B("ssmp_t")]))
        o.append(lambda: dma(lambda e, st=st: e.dma_start(out=ssmp_in[st].ap(), in_=ssmp_t[:, :]),
            [B("ssmp_t")], [B(f"ssmp_in{st}")], "ssmp_st"))
        o.append(lambda: vop(lambda v: v.tensor_copy(xin_str[:, :, 0:3], xin_str[:, :, STR:STR + 3]),
            [B(f"xin{c}") for c in range(NCH)], [B(f"xin{c}") for c in range(NCH)], "halo"))
        return o

    def gen_A2(st):
        if st >= NSTR:
            return None, []
        p = st % 2
        xact_str = xact2[p]
        ar = lambda: fw.collective(
            lambda g, st=st: g.collective_compute("AllReduce", ALU.add, replica_groups=PAIRS,
                                                  ins=[ssmp_in[st].ap().opt()],
                                                  outs=[ssmp_out[st].ap().opt()]),
            reads=[B(f"ssmp_in{st}")], writes=[B(f"ssmp_out{st}")], tag="ssmp_ar")
        o = []
        o.append(lambda: dma(lambda e, st=st: e.dma_start(out=ssmp_t[:, :], in_=ssmp_out[st].ap()),
            [B("ssmp_t"), B(f"ssmp_out{st}")], [B("ssmp_t")], "ssmp_ld"))
        o.append(lambda: vop(lambda v: v.tensor_copy(dtr_b[:, :], ssmp_t[0:RNK, :]),
                             [B("ssmp_t")], [B("dtr_b")]))
        for tc in range(2):
            o.append(lambda tc=tc: top(
                lambda t, tc=tc: t.transpose(p7[:, 0:32], ssmp_t[64:96, bass.ts(tc, 128)],
                                             identf[64:96, 64:96]),
                [B("ssmp_t"), B("identf"), B("p7"), B("p7")], [B("p7")], "bcT"))
            o.append(lambda tc=tc, p=p: sop(
                lambda s, tc=tc: s.copy(out=BCt2[p][:, tc, :], in_=p7[:, 0:32]),
                [B("p7")], [B(f"BCt{p}")], "bcTe"))
        return ar, o

    def gen_PRO(st):
        if st >= NSTR:
            return []
        p = st % 2
        xact_str = xact2[p]
        o = []
        for tc in range(2):
            for hh in range(2):
                pb, pbuf = (p7, B("p7")) if hh == 0 else (yTh[0], B("yT0"))
                o.append(lambda tc=tc, hh=hh, pb=pb, pbuf=pbuf: top(
                    lambda t, tc=tc, hh=hh, pb=pb: t.matmul(
                        pb[:, :], dtr_b[:, bass.ts(tc, 128)], w_dt_s[:, bass.ts(hh, 512)],
                        start=True, stop=True),
                    [B("wdts"), B("dtr_b"), pbuf], [pbuf], "dt_mm"))
                o.append(lambda hh=hh, pb=pb, pbuf=pbuf: vop(
                    lambda v, hh=hh, pb=pb: v.tensor_tensor(
                        egp[0][:, bass.ts(hh, 512)], pb[:, :], bdtbc[:, bass.ts(hh, 512)], ALU.add),
                    [pbuf, B("bdtbc")], [B("eg0")], "dt_badd"))
                o.append(lambda hh=hh: sop(
                    lambda s, hh=hh: s.activation(egp[1][:, bass.ts(hh, 512)],
                                                  egp[0][:, bass.ts(hh, 512)], AF.Exp),
                    [B("eg0")], [B("eg1")], "dt_exp"))
                o.append(lambda hh=hh: sop(
                    lambda s, hh=hh: s.activation(dtTM[:, bass.ts(hh, 512)],
                                                  egp[1][:, bass.ts(hh, 512)], AF.Ln, bias=1.0),
                    [B("eg1")], [B("dtTM")], "dt_ln"))
            for r in range(2):
                pb, pbuf = (p7, B("p7")) if r == 0 else (yTh[1], B("yT1"))
                o.append(lambda r=r, pb=pb, pbuf=pbuf: top(
                    lambda t, r=r, pb=pb: t.matmul(pb[:, :], trif[:, :], dtTM[:, bass.ts(r, 512)],
                                                   start=True, stop=True),
                    [B("trif"), B("dtTM"), pbuf], [pbuf], "pcum"))
                o.append(lambda r=r, tc=tc, pb=pb, pbuf=pbuf: sop(
                    lambda s, r=r, tc=tc, pb=pb: s.copy(out=phs[tc][:, bass.ts(r, 512)], in_=pb[:, :]),
                    [pbuf], [B(f"phs{tc}")], "pcume"))
            for r in range(2):
                pb, pbuf = (p7, B("p7")) if r == 0 else (yTh[0], B("yT0"))
                for cb in range(4):
                    c = r * 4 + cb
                    o.append(lambda tc=tc, c=c, cb=cb, p=p, pb=pb, pbuf=pbuf, xact_str=xact_str: top(
                        lambda t, c=c, cb=cb, tc=tc, pb=pb, xact_str=xact_str: t.transpose(
                            pb[:, bass.ts(cb, 128)], xact_str[:, c, bass.ts(tc, 128)], identf[:, :]),
                        [B(f"xact{c}_{p}"), B("identf"), pbuf], [pbuf], "tmTu"))
                o.append(lambda tc=tc, r=r, pb=pb, pbuf=pbuf: sop(
                    lambda s, tc=tc, r=r, pb=pb: s.copy(out=uTM[tc][:, bass.ts(r, 512)], in_=pb[:, :]),
                    [pbuf], [B(f"uTM{tc}")], "tmTue"))
            o.append(lambda tc=tc: vop(
                lambda v, tc=tc: v.tensor_tensor(uTM[tc][:, :], uTM[tc][:, :], dtTM[:, :], ALU.mult),
                [B(f"uTM{tc}"), B("dtTM")], [B(f"uTM{tc}")], "uTMm"))
        return o

    def gen_C(st):
        o = []
        p = st % 2
        sz_str = sz_str2[p]
        szb = B(f"sz_str{p}")
        xdb = xdb2[p]
        ychm = xact2[p]
        blk = st // 8
        win_i = st % 8
        for c in range(NCH):
            o.append(lambda c=c, p=p, xdb=xdb, ychm=ychm: vop(
                lambda v, c=c, xdb=xdb, ychm=ychm: v.tensor_tensor(
                    qt_t2[c % 2][:, :], xdb[:, c, :], ychm[:, c, :], ALU.add),
                [B(f"xdb{c}_{p}"), B(f"xact{c}_{p}")], [B(f"qt{c % 2}")], "qt"))
            o.append(lambda c=c, sz_str=sz_str, szb=szb: vop(
                lambda v, c=c, sz_str=sz_str: v.tensor_tensor(q_str[:, c, :], qt_t2[c % 2][:, :],
                                                              sz_str[:, c, :], ALU.mult),
                [B(f"qt{c % 2}"), szb], [B("q_str")], "q"))
        for dcol in range(NDC):
            pb, pbuf = (p0, B("p0")) if dcol % 2 == 0 else (p1, B("p1"))
            af = attf2[dcol % 2]
            afb = B(f"attf{dcol % 2}")
            o.append(lambda dcol=dcol, pb=pb, pbuf=pbuf: top(
                lambda t, dcol=dcol, pb=pb: [
                    t.matmul(pb[:, 0:STR], w_out_s[:, c, bass.ts(dcol, 128)], q_str[:, c, :],
                             start=(c == 0), stop=(c == NCH - 1)) for c in range(NCH)][-1],
                [B("wouts"), B("q_str"), pbuf], [pbuf], "wout_mm"))
            o.append(lambda dcol=dcol, pb=pb, pbuf=pbuf, af=af, afb=afb: vop(
                lambda v, dcol=dcol, pb=pb, af=af: v.tensor_scalar(
                    af[:, :], pb[:, 0:STR], gacol[:, dcol:dcol + 1], None, op0=ALU.mult),
                [pbuf, B("gacol")], [afb], "ga"))
            for ssub in range(2):
                o.append(lambda ssub=ssub, af=af, afb=afb: top(
                    lambda t, ssub=ssub, af=af: t.transpose(p6b[:, bass.ts(ssub, 128)],
                                                            af[:, bass.ts(ssub, 128)], identb[:, :]),
                    [afb, B("identb")], [B("p6b")], "attT"))
            o.append(lambda dcol=dcol: sop(
                lambda s, dcol=dcol: s.copy(
                    out=attnTM[:, :, bass.ts(dcol, 128)],
                    in_=p6b[:, 0:256].rearrange("p (a b) -> p a b", a=2)),
                [B("p6b")], [B("attnTM")], "attTM"))
            o.append(lambda dcol=dcol, win_i=win_i, blk=blk: dma(
                lambda e, win_i=win_i, blk=blk, dcol=dcol: e.dma_start(
                    out=att_in[win_i].ap()[blk][:, bass.ts(dcol, 128)].rearrange(
                        "(a p) d -> p a d", p=128),
                    in_=attnTM[:, :, bass.ts(dcol, 128)]),
                [B("attnTM")], [B(f"att_in{win_i}_{blk}")], "att_st"))
        return o

    def gen_RS(w):
        return lambda: fw.collective(
            lambda g, w=w: g.collective_compute("ReduceScatter", ALU.add, replica_groups=PAIRS,
                                                ins=[att_in[w].ap().opt()],
                                                outs=[att_out[w].ap().opt()]),
            reads=[B(f"att_in{w}_0"), B(f"att_in{w}_1")],
            writes=[B(f"att_out{w}")], tag="att_rs")

    for f_ in gen_A1(0):
        f_()
    ar0, a2_0 = gen_A2(0)
    ar0()
    for f_ in a2_0:
        f_()
    for f_ in gen_PRO(0):
        f_()
    pending = list(gen_A1(1))
    nxt_ar, nxt_a2 = gen_A2(1)

    PIPE = True
    for st in range(NSTR):
        p = st % 2
        if not PIPE:
            while pending:
                pending.pop(0)()
            if nxt_ar is not None:
                nxt_ar()
                nxt_ar = None
            if st >= 9:
                gen_RS(st - 9)()
        per = max(1, -(-len(pending) // 30))
        for tc in range(2):
            g = st * 2 + tc
            for kk in range(KS + 2):
                if not PIPE:
                    pass
                elif tc == 1 and kk == KS:
                    if nxt_ar is not None:
                        nxt_ar()
                        nxt_ar = None
                elif tc == 1 and kk == KS + 1:
                    if st >= 9:
                        gen_RS(st - 9)()
                else:
                    for _ in range(per):
                        if pending:
                            pending.pop(0)()
        for tc in range(2):
            g = st * 2 + tc
            for kk in range(KS + 2):
                if kk < KS:
                    k = kk
                    esl, wvi = k % 2, k % 4
                    sop(lambda s, k=k, esl=esl, tc=tc: s.activation(
                            egp[esl][:, :], phs[tc][:, :], AF.Exp, scale=nakb[:, k:k + 1]),
                        [B(f"phs{tc}"), B("nakb")], [B(f"eg{esl}")], "Ek")
                    for hf in range(2):
                        vop(lambda v, k=k, esl=esl, wvi=wvi, hf=hf, tc=tc, p=p: v.scalar_tensor_tensor(
                                wv[wvi][:, bass.ts(hf, 512)], egp[esl][:, bass.ts(hf, 512)],
                                BCt2[p][:, tc, k:k + 1], uTM[tc][:, bass.ts(hf, 512)],
                                ALU.mult, ALU.mult),
                            [B(f"eg{esl}"), B(f"BCt{p}"), B(f"uTM{tc}")], [B(f"wv{wvi}")], "vstt")
                    if g > 0:
                        fw.dma("gpsimd", (lambda k=k, wvi=wvi: (lambda e: e.dma_start(
                                out=wv[wvi][0:1, :], in_=crow[k:k + 1, :], accum_op=ALU.add)))(),
                               [B(f"crow{k}"), B(f"wv{wvi}")], [B(f"wv{wvi}")], "cadd")
                if 2 <= kk:
                    k = kk - 2
                    gsl, wti = 2 + (k % 2), k % 4
                    for hf in range(2):
                        vop(lambda v, gsl=gsl, wti=wti, hf=hf: v.tensor_tensor(
                                wt[wti][:, bass.ts(hf, 512)], egp[gsl][:, bass.ts(hf, 512)],
                                pHh[hf][:, :], ALU.mult),
                            [B(f"eg{gsl}"), B(f"pH{hf}")], [B(f"wt{wti}")], "ttt")
                    if g < 31:
                        fw.dma("sync", (lambda k=k, wti=wti: (lambda e: e.dma_start(
                                out=crow[k:k + 1, :], in_=wt[wti][127:128, :])))(),
                               [B(f"wt{wti}")], [B(f"crow{k}"), B("crowA")], "csave")
                    for hf in range(2):
                        top(lambda t, k=k, wti=wti, hf=hf, di=k % 2: t.matmul(
                                yTh[hf][:, :], diag[di][:, :], wt[wti][:, bass.ts(hf, 512)],
                                start=(k == 0), stop=(k == KS - 1)),
                            [B(f"diag{k % 2}"), B(f"wt{wti}"), B(f"yT{hf}")], [B(f"yT{hf}")], "red")
                if 1 <= kk <= KS:
                    k = kk - 1
                    gsl, wvi = 2 + (k % 2), k % 4
                    for hf in range(2):
                        reads = [B("trib"), B(f"wv{wvi}"), B(f"pH{hf}")]
                        top(lambda t, wvi=wvi, hf=hf: t.matmul(
                                pHh[hf][:, :], trib[:, :], wv[wvi][:, bass.ts(hf, 512)],
                                start=True, stop=True),
                            reads, [B(f"pH{hf}")], "tri")
                    sop(lambda s, k=k, gsl=gsl, tc=tc: s.activation(
                            egp[gsl][:, :], phs[tc][:, :], AF.Exp, scale=akb[:, k:k + 1]),
                        [B(f"phs{tc}"), B("akb")], [B(f"eg{gsl}")], "Gk")
                    sop(lambda s, k=k, tc=tc, di=k % 2, p=p: s.activation(
                            diag[di][:, :], identb[:, :], AF.Copy, scale=BCt2[p][:, tc, KS + k:KS + k + 1]),
                        [B("identb"), B(f"BCt{p}")], [B(f"diag{k % 2}")], "diagC")
            sop(lambda s: s.copy(out=ySBb[:, 0:512], in_=yTh[0][:, :]),
                [B("yT0"), B("ySBb")], [B("ySBb")], "yev0")
            sop(lambda s: s.copy(out=ySBb[:, 512:1024], in_=yTh[1][:, :]),
                [B("yT1"), B("ySBb")], [B("ySBb")], "yev1")
            for c in range(NCH):
                top(lambda t, c=c: t.transpose(p6b[:, bass.ts(c, 128)], ySBb[:, bass.ts(c, 128)],
                                               identb[:, :]),
                    [B("ySBb"), B("identb"), B("p6b")], [B("p6b")], "yT2")
            sop(lambda s, tc=tc, p=p: s.copy(
                    out=xact2[p][:, :, bass.ts(tc, 128)],
                    in_=p6b.ap().rearrange("p (a b) -> p a b", a=8)),
                [B("p6b")] + [B(f"xact{i}_{p}") for i in range(NCH)],
                [B(f"xact{i}_{p}") for i in range(NCH)], "yT2e")
        while pending:
            pending.pop(0)()
        for f_ in nxt_a2:
            f_()
        nxt_ar, nxt_a2 = gen_A2(st + 2)
        for f_ in gen_PRO(st + 1):
            f_()
        pending = gen_C(st) + gen_A1(st + 2)
    for f_ in pending:
        f_()
    gen_RS(7)()



    for w in range(NW):
        for hw in range(2):
            r0 = w * 256 + hw * 128
            dma(lambda e, w=w, hw=hw: e.dma_start(
                    out=rs_t.ap(),
                    in_=att_out[w].ap()[bass.ts(hw, 128), :].rearrange("(a p) d -> p a d", p=128)),
                [B("rs_t"), B(f"att_out{w}")], [B("rs_t")], "rs_ld")
            dma(lambda e, r0=r0: e.dma_start(
                    out=uTM[1][:, :], in_=x_own.ap()[r0:r0 + 128, :]),
                [B("uTM1")], [B("uTM1")], "xo_ld")
            vop(lambda v: v.tensor_tensor(uTM[1][:, :], uTM[1][:, :], rs_t[:, 0, :], ALU.add),
                [B("uTM1"), B("rs_t")], [B("uTM1")], "xres")
            dma(lambda e, r0=r0: e.dma_start(
                    out=xres_b.ap()[r0:r0 + 128, :], in_=uTM[1][:, :]),
                [B("uTM1")], [B("xres_bd")], "xres_st")

    fw.phase_barrier()
    mam.close()

    ffn = contextlib.ExitStack()
    xr_t2 = [sb(f"xr_t{i}", [128, D], F32, ffn) for i in range(2)]
    st8_2 = [sb(f"st8_{i}", [128, 8], F32, ffn) for i in range(2)]
    mmacc = sb("mmacc", [128, 4], F32, ffn)
    mm2 = sb("mm2", [1, 2], F32, ffn)
    i32s = sb("i32s", [1, 1], I32, ffn)
    wld2 = [sb(f"wld{i}", [128, 512], F32, ffn) for i in range(2)]
    wi322 = [sb(f"wi32{i}", [128, 512], I32, ffn) for i in range(2)]
    wqb2 = [sb(f"wqb{i}", [128, 512], BF16, ffn) for i in range(2)]
    wq2T = sb("wq2T", [128, NHC, D], BF16, ffn)
    cs1 = sb("cs1", [128, NHC], F32, ffn)
    cs2 = sb("cs2", [128, NDC], F32, ffn)
    be1 = sb("be1", [128, NHC], F32, ffn)
    be2 = sb("be2", [128, NDC], F32, ffn)
    b1c_t = sb("b1c_t", [128, NHC], F32, ffn)
    b2c_t = sb("b2c_t", [128, NDC], F32, ffn)
    g_t2 = [sb(f"g_t{i}", [128, 512], F32, ffn) for i in range(2)]
    red12 = [sb(f"red1{i}", [128, 1], F32, ffn) for i in range(2)]
    ffnA = contextlib.ExitStack()
    h2_t2 = [sb(f"h2_t{i}", [128, D], F32, ffnA) for i in range(2)]
    lst = sb("lst", [128, 2 * (SH // 128)], F32, ffnA)
    capb_bc = sb("capb_bc", [128, D], F32, ffnA)
    sb_bc = sb("sb_bc", [128, D], F32, ffnA)
    i32t2 = [sb(f"i32t{i}", [128, D], I32, ffnA) for i in range(2)]
    xq_tm2 = [sb(f"xq_tm{i}", [128, D], BF16, ffnA) for i in range(2)]
    xqT = sb("xqT", [128, NDC, SH], BF16, ffnA)
    wq1c2 = [sb(f"wq1c{i}", [128, NDC, 128], BF16, ffnA) for i in range(2)]

    dma(lambda e: e.dma_start(out=b1c_t.ap(), in_=b1_c.ap()), [], [B("b1c")])
    dma(lambda e: e.dma_start(out=b2c_t.ap(), in_=b2_c.ap()), [], [B("b2c")])
    vop(lambda v: v.tensor_scalar(i32s[0:1, 0:1], constc[0:1, 3:4], 1.0, None, op0=ALU.mult),
        [B("constc")], [B("i32s")], "probe")
    vop(lambda v: v.tensor_scalar(scal[0:1, 22:23], i32s[0:1, 0:1], -0.5, 0.5,
                                  op0=ALU.mult, op1=ALU.add),
        [B("i32s")], [B("scal")], "moff")
    vop(lambda v: v.tensor_scalar(scal[0:1, 23:24], scal[0:1, 22:23], 256.0, None, op0=ALU.add),
        [B("scal")], [B("scal")], "rndc")
    top(lambda t: t.matmul(p7[:, 0:1], onesrf[:, :], scal[0:1, 23:24], start=True, stop=True),
        [B("scal"), B("onesrf"), B("p7")], [B("p7")], "rndc_bc")
    sop(lambda s: s.copy(out=scalb[:, 10:11], in_=p7[:, 0:1]), [B("p7")], [B("scalb")], "rndc_c")
    vop(lambda v: v.memset(mmacc[:, 0:1], -3.0e38), [], [B("mmacc")])
    vop(lambda v: v.memset(mmacc[:, 1:2], 3.0e38), [B("mmacc")], [B("mmacc")])
    vop(lambda v: v.memset(mmacc[:, 2:3], -3.0e38), [B("mmacc")], [B("mmacc")])
    vop(lambda v: v.memset(mmacc[:, 3:4], 3.0e38), [B("mmacc")], [B("mmacc")])

    rowst = sb("rowst", [1, 128], F32, ffnA)
    for (base, dstb, dst, extra) in ((32, "capb_bc", capb_bc, 1.0), (24, "sb_bc", sb_bc, 0.0)):
        for i in range(NDC):
            top(lambda t, base=base, i=i: t.transpose(p7[0:1, 0:128], modT[:, base + i:base + i + 1],
                                                      identf[:, :]),
                [B("modT"), B("identf"), B("p7")], [B("p7")], "cbT")
            sop(lambda s: s.copy(out=rowst[:, :], in_=p7[0:1, 0:128]), [B("p7")], [B("rowst")])
            top(lambda t: t.matmul(p7[:, 0:128], onesrf[:, :], rowst[:, :], start=True, stop=True),
                [B("onesrf"), B("rowst"), B("p7")], [B("p7")], "cbbc")
            if extra:
                vop(lambda v, i=i, dst=dst: v.tensor_scalar(dst[:, bass.ts(i, 128)], p7[:, 0:128],
                                                            1.0, None, op0=ALU.add),
                    [B("p7")], [B(dstb)])
            else:
                sop(lambda s, i=i, dst=dst: s.copy(out=dst[:, bass.ts(i, 128)], in_=p7[:, 0:128]),
                    [B("p7")], [B(dstb)])

    for (wten, col, tag) in ((w1t, 0, "w1a"), (w2t, 1, "w2a")):
        n_r, n_c = (NDC, 8) if col == 0 else (NHC, 2)
        first = True
        for rr in range(n_r):
            for cq in range(n_c):
                bi = (rr * n_c + cq) % 2
                wld, red1 = wld2[bi], red12[bi]
                dma(lambda e, rr=rr, cq=cq, wld=wld, wten=wten: e.dma_start(
                        out=wld[:, :], in_=wten.ap()[bass.ts(rr, 128), bass.ts(cq, 512)]),
                    [B(f"wld{bi}")], [B(f"wld{bi}")], tag + "_ld")
                vop(lambda v, wld=wld, red1=red1: v.tensor_reduce(
                        red1[:, :], wld[:, :], AX.X, ALU.max, apply_absolute_value=True),
                    [B(f"wld{bi}")], [B(f"red1{bi}")], tag + "_r")
                if first:
                    vop(lambda v, red1=red1, col=col: v.tensor_copy(st8_2[0][:, col:col + 1], red1[:, :]),
                        [B(f"red1{bi}")], [B("st8_0")], tag + "_a0")
                    first = False
                else:
                    vop(lambda v, red1=red1, col=col: v.tensor_tensor(
                            st8_2[0][:, col:col + 1], st8_2[0][:, col:col + 1], red1[:, :], ALU.max),
                        [B(f"red1{bi}"), B("st8_0")], [B("st8_0")], tag + "_a")
        top(lambda t, col=col: t.transpose(p7[0:1, 0:128], st8_2[0][:, col:col + 1], identf[:, :]),
            [B("st8_0"), B("identf"), B("p7")], [B("p7")])
        vop(lambda v, col=col: v.tensor_reduce(scal[0:1, 16 + col:17 + col], p7[0:1, 0:128], AX.X, ALU.max),
            [B("p7")], [B("scal")])
    vop(lambda v: v.tensor_scalar(scal[0:1, 5:6], scal[0:1, 16:17], 1.0 / 127.0, 1e-8,
                                  op0=ALU.mult, op1=ALU.max), [B("scal")], [B("scal")])
    vop(lambda v: v.tensor_scalar(scal[0:1, 13:14], scal[0:1, 17:18], 1.0 / 127.0, 1e-8,
                                  op0=ALU.mult, op1=ALU.max), [B("scal")], [B("scal")])
    vop(lambda v: v.reciprocal(scal[0:1, 6:7], scal[0:1, 5:6]), [B("scal")], [B("scal")])
    vop(lambda v: v.reciprocal(scal[0:1, 14:15], scal[0:1, 13:14]), [B("scal")], [B("scal")])
    for (srcslot, col) in ((6, 8), (14, 9)):
        top(lambda t, srcslot=srcslot: t.matmul(p7[:, 0:1], onesrf[:, :], scal[0:1, srcslot:srcslot + 1],
                                                start=True, stop=True),
            [B("scal"), B("onesrf"), B("p7")], [B("p7")], "rswbc")
        sop(lambda s, col=col: s.copy(out=scalb[:, col:col + 1], in_=p7[:, 0:1]),
            [B("p7")], [B("scalb")])

    for hg in range(8):
        for dc in range(NDC):
            bi = (hg * NDC + dc) % 2
            wld, wi32, wqb = wld2[bi], wi322[bi], wqb2[bi]
            dma(lambda e, hg=hg, dc=dc, wld=wld: e.dma_start(
                    out=wld[:, :], in_=w1t.ap()[bass.ts(dc, 128), bass.ts(hg, 512)]),
                [B(f"wld{bi}")], [B(f"wld{bi}")], "w1q_ld")
            vop(lambda v, wld=wld, wi32=wi32: v.tensor_scalar(
                    wi32[:, :], wld[:, :], scalb[:, 8:9], scalb[:, 10:11],
                    op0=ALU.mult, op1=ALU.add),
                [B(f"wld{bi}"), B("scalb")], [B(f"wi32{bi}")], "w1q_i")
            vop(lambda v, wld=wld, wi32=wi32: v.tensor_scalar(
                    wld[:, :], wi32[:, :], 256.0 + 127.0, 256.0 - 127.0, op0=ALU.min, op1=ALU.max),
                [B(f"wi32{bi}")], [B(f"wld{bi}")], "w1q_c")
            vop(lambda v, wld=wld, wqb=wqb: v.tensor_scalar(
                    wqb[:, :], wld[:, :], -256.0, None, op0=ALU.add),
                [B(f"wld{bi}")], [B(f"wqb{bi}")], "w1q_s")
            dma(lambda e, hg=hg, dc=dc, wqb=wqb: e.dma_start(
                    out=wq1_b.ap()[bass.ts(dc, 128), bass.ts(hg, 512)], in_=wqb[:, :]),
                [B(f"wqb{bi}")], [B("wq1_bd")], "w1q_st")
            csb = [pHh[0], pHh[1], yTh[0], yTh[1]]
            top(lambda t, dc=dc, wqb=wqb, csb=csb: [
                    t.matmul(csb[q][:, 0:1], wqb[:, bass.ts(q, 128)], onescb[:, :],
                             start=(dc == 0), stop=(dc == NDC - 1)) for q in range(4)][-1],
                [B(f"wqb{bi}"), B("onescb"), B("pH0"), B("pH1"), B("yT0"), B("yT1")],
                [B("pH0"), B("pH1"), B("yT0"), B("yT1")], "cs1mm")
        for q in range(4):
            csb = [pHh[0], pHh[1], yTh[0], yTh[1]]
            sop(lambda s, hg=hg, q=q, csb=csb: s.copy(out=cs1[:, hg * 4 + q:hg * 4 + q + 1],
                                                      in_=csb[q][:, 0:1]),
                [B("pH0"), B("pH1"), B("yT0"), B("yT1")], [B("cs1")], "cs1ev")

    for hc in range(NHC):
        for g2 in range(2):
            bi = (hc * 2 + g2) % 2
            wld, wi32 = wld2[bi], wi322[bi]
            dma(lambda e, hc=hc, g2=g2, wld=wld: e.dma_start(
                    out=wld[:, :], in_=w2t.ap()[bass.ts(hc, 128), bass.ts(g2, 512)]),
                [B(f"wld{bi}")], [B(f"wld{bi}")], "w2q_ld")
            vop(lambda v, wld=wld, wi32=wi32: v.tensor_scalar(
                    wi32[:, :], wld[:, :], scalb[:, 9:10], scalb[:, 10:11],
                    op0=ALU.mult, op1=ALU.add),
                [B(f"wld{bi}"), B("scalb")], [B(f"wi32{bi}")], "w2q_i")
            vop(lambda v, wld=wld, wi32=wi32: v.tensor_scalar(
                    wld[:, :], wi32[:, :], 256.0 + 127.0, 256.0 - 127.0, op0=ALU.min, op1=ALU.max),
                [B(f"wi32{bi}")], [B(f"wld{bi}")], "w2q_c")
            vop(lambda v, hc=hc, g2=g2, wld=wld: v.tensor_scalar(
                    wq2T[:, hc, bass.ts(g2, 512)], wld[:, :], -256.0, None, op0=ALU.add),
                [B(f"wld{bi}")], [B("wq2T")], "w2q_s")
    for dcol in range(NDC):
        top(lambda t, dcol=dcol: [
                t.matmul(p7[:, NHC + dcol:NHC + dcol + 1], wq2T[:, hc, bass.ts(dcol, 128)],
                         onescb[:, :], start=(hc == 0), stop=(hc == NHC - 1))
                for hc in range(NHC)][-1],
            [B("wq2T"), B("onescb"), B("p7")], [B("p7")], "cs2mm")
    sop(lambda s: s.copy(out=cs2[:, :], in_=p7[:, NHC:NHC + NDC]), [B("p7")], [B("cs2")])

    NT = SH // 128
    for t_i in range(NT):
        bi = t_i % 2
        xr_t, h2_t, st8, red1 = xr_t2[bi], h2_t2[bi], st8_2[bi], red12[bi]
        xrb, h2b, stb, rdb = B(f"xr_t{bi}"), B(f"h2_t{bi}"), B(f"st8_{bi}"), B(f"red1{bi}")
        dma(lambda e, t_i=t_i, xr_t=xr_t: e.dma_start(out=xr_t[:, :], in_=xres_b.ap()[bass.ts(t_i, 128), :]),
            [xrb, B("xres_bd")], [xrb], "pa_ld")
        vop(lambda v, xr_t=xr_t, st8=st8: v.tensor_reduce(st8[:, 0:1], xr_t[:, :], AX.X, ALU.add),
            [xrb], [stb])
        sop(lambda s, xr_t=xr_t, h2_t=h2_t, st8=st8: s.activation(h2_t[:, :], xr_t[:, :], AF.Square,
                                                                  accum_out=st8[:, 1:2]),
            [xrb], [h2b, stb])
        vop(lambda v, st8=st8: v.tensor_scalar(st8[:, 2:3], st8[:, 0:1], 1.0 / D, None, op0=ALU.mult),
            [stb], [stb])
        sop(lambda s, st8=st8: s.activation(st8[:, 3:4], st8[:, 2:3], AF.Square), [stb], [stb])
        vop(lambda v, st8=st8: v.tensor_scalar(st8[:, 4:5], st8[:, 1:2], 1.0 / D, st8[:, 3:4],
                                               op0=ALU.mult, op1=ALU.subtract), [stb], [stb])
        sop(lambda s, st8=st8: s.activation(st8[:, 5:6], st8[:, 4:5], AF.Sqrt, bias=1e-5),
            [stb], [stb])
        vop(lambda v, st8=st8: v.reciprocal(st8[:, 6:7], st8[:, 5:6]), [stb], [stb])
        vop(lambda v, t_i=t_i, st8=st8: v.tensor_copy(lst[:, 2 * t_i:2 * t_i + 1], st8[:, 2:3]),
            [stb], [B("lst")])
        vop(lambda v, t_i=t_i, st8=st8: v.tensor_copy(lst[:, 2 * t_i + 1:2 * t_i + 2], st8[:, 6:7]),
            [stb], [B("lst")])
        vop(lambda v, xr_t=xr_t, h2_t=h2_t, st8=st8: v.tensor_scalar(
                h2_t[:, :], xr_t[:, :], st8[:, 2:3], st8[:, 6:7], op0=ALU.subtract, op1=ALU.mult),
            [xrb, stb], [h2b])
        vop(lambda v, h2_t=h2_t: v.tensor_tensor(h2_t[:, :], h2_t[:, :], capb_bc[:, :], ALU.mult),
            [h2b, B("capb_bc")], [h2b])
        vop(lambda v, h2_t=h2_t: v.tensor_tensor(h2_t[:, :], h2_t[:, :], sb_bc[:, :], ALU.add),
            [h2b, B("sb_bc")], [h2b])
        vop(lambda v, h2_t=h2_t, red1=red1: v.tensor_reduce(red1[:, :], h2_t[:, :], AX.X, ALU.max),
            [h2b], [rdb])
        vop(lambda v, red1=red1: v.tensor_tensor(mmacc[:, 0:1], mmacc[:, 0:1], red1[:, :], ALU.max),
            [rdb, B("mmacc")], [B("mmacc")])
        vop(lambda v, h2_t=h2_t, red1=red1: v.tensor_reduce(red1[:, :], h2_t[:, :], AX.X, ALU.min),
            [h2b], [rdb])
        vop(lambda v, red1=red1: v.tensor_tensor(mmacc[:, 1:2], mmacc[:, 1:2], red1[:, :], ALU.min),
            [rdb, B("mmacc")], [B("mmacc")])
    top(lambda t: t.transpose(p7[0:1, 0:128], mmacc[:, 0:1], identf[:, :]),
        [B("mmacc"), B("identf")], [B("p7")])
    vop(lambda v: v.tensor_reduce(mm2[0:1, 0:1], p7[0:1, 0:128], AX.X, ALU.max),
        [B("p7")], [B("mm2")])
    top(lambda t: t.transpose(p7[0:1, 0:128], mmacc[:, 1:2], identf[:, :]),
        [B("mmacc"), B("identf"), B("p7")], [B("p7")])
    vop(lambda v: v.tensor_reduce(mm2[0:1, 1:2], p7[0:1, 0:128], AX.X, ALU.min),
        [B("p7")], [B("mm2")])
    vop(lambda v: v.tensor_scalar(mm2[0:1, 1:2], mm2[0:1, 1:2], -1.0, None, op0=ALU.mult),
        [B("mm2")], [B("mm2")])
    dma(lambda e: e.dma_start(out=mmx_in.ap(), in_=mm2[0:1, :]), [B("mm2")], [B("mmx_in")], "mmx_st")
    fw.collective(lambda g: g.collective_compute("AllReduce", ALU.max, replica_groups=ALL8,
                                                 ins=[mmx_in.ap().opt()], outs=[mmx_out.ap().opt()]),
                  reads=[B("mmx_in")], writes=[B("mmx_out")], tag="mmx_ar")
    dma(lambda e: e.dma_start(out=mm2[0:1, :], in_=mmx_out.ap()),
        [B("mm2"), B("mmx_out")], [B("mm2")], "mmx_ld")

    vop(lambda v: v.tensor_tensor(scal[0:1, 20:21], mm2[0:1, 0:1], mm2[0:1, 1:2], ALU.add),
        [B("mm2")], [B("scal")], "q1_rng")
    vop(lambda v: v.tensor_scalar(scal[0:1, 2:3], scal[0:1, 20:21], 1.0 / 255.0, 1e-8,
                                  op0=ALU.mult, op1=ALU.max), [B("scal")], [B("scal")], "q1_s")
    vop(lambda v: v.reciprocal(scal[0:1, 3:4], scal[0:1, 2:3]), [B("scal")], [B("scal")], "q1_rs")
    vop(lambda v: v.tensor_tensor(scal[0:1, 18:19], mm2[0:1, 1:2], scal[0:1, 3:4], ALU.mult),
        [B("mm2"), B("scal")], [B("scal")], "q1_z0")
    sop(lambda s: s.activation(i32s[0:1, 0:1], scal[0:1, 18:19], AF.Copy, bias=RND),
        [B("scal"), B("constc")], [B("i32s")], "q1_zi")
    vop(lambda v: v.tensor_scalar(scal[0:1, 18:19], i32s[0:1, 0:1], 256.0 + 255.0, 256.0,
                                  op0=ALU.min, op1=ALU.max), [B("i32s")], [B("scal")], "q1_zc")
    vop(lambda v: v.tensor_scalar(scal[0:1, 4:5], scal[0:1, 18:19], -256.0 - 128.0, None,
                                  op0=ALU.add), [B("scal")], [B("scal")], "q1_zs")
    vop(lambda v: v.tensor_tensor(scal[0:1, 7:8], scal[0:1, 2:3], scal[0:1, 5:6], ALU.mult),
        [B("scal")], [B("scal")], "q1_sx")
    vop(lambda v: v.tensor_tensor(scal[0:1, 19:20], scal[0:1, 4:5], scal[0:1, 7:8], ALU.mult),
        [B("scal")], [B("scal")], "q1_zx")
    vop(lambda v: v.tensor_scalar(scal[0:1, 24:25], scal[0:1, 4:5], 1.0, scal[0:1, 23:24],
                                  op0=ALU.mult, op1=ALU.add),
        [B("scal")], [B("scal")], "q1_zpr")
    for (srcslot, col, extra) in ((3, 0, 0.0), (24, 1, 0.0), (7, 2, 0.0), (19, 3, 0.0)):
        top(lambda t, srcslot=srcslot: t.matmul(p7[:, 0:1], onesrf[:, :],
                                                scal[0:1, srcslot:srcslot + 1],
                                                start=True, stop=True),
            [B("scal"), B("onesrf"), B("p7")], [B("p7")], "q1_bc")
        if extra:
            vop(lambda v, col=col, extra=extra: v.tensor_scalar(
                    scalb[:, col:col + 1], p7[:, 0:1], extra, None, op0=ALU.add),
                [B("p7")], [B("scalb")], "q1_bca")
        else:
            sop(lambda s, col=col: s.copy(out=scalb[:, col:col + 1], in_=p7[:, 0:1]),
                [B("p7")], [B("scalb")], "q1_bcc")
    vop(lambda v: v.scalar_tensor_tensor(be1[:, :], cs1[:, :], scalb[:, 3:4], b1c_t[:, :],
                                         ALU.mult, ALU.subtract),
        [B("cs1"), B("scalb"), B("b1c")], [B("be1")], "be1")
    vop(lambda v: v.tensor_scalar(be1[:, :], be1[:, :], -1.0, None, op0=ALU.mult),
        [B("be1")], [B("be1")], "be1n")

    for t_i in range(NT):
        bi = t_i % 2
        xr_t, h2_t, i32t, xq_tm = xr_t2[bi], h2_t2[bi], i32t2[bi], xq_tm2[bi]
        xrb, h2b, i3b, xqb = B(f"xr_t{bi}"), B(f"h2_t{bi}"), B(f"i32t{bi}"), B(f"xq_tm{bi}")
        dma(lambda e, t_i=t_i, xr_t=xr_t: e.dma_start(out=xr_t[:, :], in_=xres_b.ap()[bass.ts(t_i, 128), :]),
            [xrb, B("xres_bd")], [xrb], "pb_ld")
        vop(lambda v, t_i=t_i, xr_t=xr_t, h2_t=h2_t: v.tensor_scalar(
                h2_t[:, :], xr_t[:, :], lst[:, 2 * t_i:2 * t_i + 1],
                lst[:, 2 * t_i + 1:2 * t_i + 2], op0=ALU.subtract, op1=ALU.mult),
            [xrb, B("lst")], [h2b])
        vop(lambda v, h2_t=h2_t: v.tensor_tensor(h2_t[:, :], h2_t[:, :], capb_bc[:, :], ALU.mult),
            [h2b, B("capb_bc")], [h2b])
        vop(lambda v, h2_t=h2_t: v.tensor_tensor(h2_t[:, :], h2_t[:, :], sb_bc[:, :], ALU.add),
            [h2b, B("sb_bc")], [h2b])
        vop(lambda v, h2_t=h2_t, i32t=i32t: v.tensor_scalar(
                i32t[:, :], h2_t[:, :], scalb[:, 0:1], scalb[:, 1:2], op0=ALU.mult, op1=ALU.add),
            [h2b, B("scalb")], [i3b], "xq_i")
        vop(lambda v, h2_t=h2_t, i32t=i32t: v.tensor_scalar(
                h2_t[:, :], i32t[:, :], 256.0 + 127.0, 256.0 - 128.0, op0=ALU.min, op1=ALU.max),
            [i3b], [h2b], "xq_c")
        vop(lambda v, h2_t=h2_t, xq_tm=xq_tm: v.tensor_scalar(
                xq_tm[:, :], h2_t[:, :], -256.0, None, op0=ALU.add),
            [h2b], [xqb], "xq_s")
        for dc in range(NDC):
            top(lambda t, dc=dc, xq_tm=xq_tm: t.transpose(
                    p6b[:, bass.ts(dc, 128)], xq_tm[:, bass.ts(dc, 128)], identb[:, :]),
                [xqb, B("identb"), B("p6b")], [B("p6b")], "xqT")
        sop(lambda s, t_i=t_i: s.copy(out=xqT[:, :, bass.ts(t_i, 128)],
                                      in_=p6b.ap().rearrange("p (c n) -> p c n", c=NDC)),
            [B("p6b")], [B("xqT")], "xqTc")

    for hc in range(NHC):
        wq1c = wq1c2[hc % 2]
        wqcb = B(f"wq1c{hc % 2}")
        dma(lambda e, hc=hc, wq1c=wq1c: e.dma_start(
                out=wq1c.ap(),
                in_=wq1_b.ap().rearrange("(c p) h -> p c h", p=128)[:, :, bass.ts(hc, 128)]),
            [wqcb, B("wq1_bd")], [wqcb], "f1_wld")
        for w in range(FW):
            pb, pbuf = (p0, B("p0")) if w % 2 == 0 else (p1, B("p1"))
            bi = (hc * FW + w) % 2
            g_t, red1 = g_t2[bi], red12[bi]
            gb_, rdb = B(f"g_t{bi}"), B(f"red1{bi}")
            top(lambda t, hc=hc, w=w, pb=pb, wq1c=wq1c: [
                    t.matmul(pb[:, :], wq1c[:, dc, :], xqT[:, dc, bass.ts(w, 512)],
                             start=(dc == 0), stop=(dc == NDC - 1)) for dc in range(NDC)][-1],
                [wqcb, B("xqT"), pbuf], [pbuf], "f1_mm")
            sop(lambda s, hc=hc, pb=pb, g_t=g_t: s.activation(g_t[:, :], pb[:, :], AF.Gelu,
                                                     scale=scalb[:, 2:3], bias=be1[:, hc:hc + 1]),
                [pbuf, B("scalb"), B("be1")], [gb_], "gelu")
            dma(lambda e, hc=hc, w=w, g_t=g_t: e.dma_start(
                    out=g_b.ap()[bass.ts(hc, 128), bass.ts(w, 512)], in_=g_t[:, :]),
                [gb_], [B("g_bd")], "g_st")
            vop(lambda v, g_t=g_t, red1=red1: v.tensor_reduce(red1[:, :], g_t[:, :], AX.X, ALU.max),
                [gb_], [rdb])
            vop(lambda v, red1=red1: v.tensor_tensor(mmacc[:, 2:3], mmacc[:, 2:3], red1[:, :], ALU.max),
                [rdb, B("mmacc")], [B("mmacc")])
            vop(lambda v, g_t=g_t, red1=red1: v.tensor_reduce(red1[:, :], g_t[:, :], AX.X, ALU.min),
                [gb_], [rdb])
            vop(lambda v, red1=red1: v.tensor_tensor(mmacc[:, 3:4], mmacc[:, 3:4], red1[:, :], ALU.min),
                [rdb, B("mmacc")], [B("mmacc")])
    top(lambda t: t.transpose(p7[0:1, 0:128], mmacc[:, 2:3], identf[:, :]),
        [B("mmacc"), B("identf"), B("p7")], [B("p7")])
    vop(lambda v: v.tensor_reduce(mm2[0:1, 0:1], p7[0:1, 0:128], AX.X, ALU.max),
        [B("p7"), B("mm2")], [B("mm2")])
    top(lambda t: t.transpose(p7[0:1, 0:128], mmacc[:, 3:4], identf[:, :]),
        [B("mmacc"), B("identf"), B("p7")], [B("p7")])
    vop(lambda v: v.tensor_reduce(mm2[0:1, 1:2], p7[0:1, 0:128], AX.X, ALU.min),
        [B("p7")], [B("mm2")])
    vop(lambda v: v.tensor_scalar(mm2[0:1, 1:2], mm2[0:1, 1:2], -1.0, None, op0=ALU.mult),
        [B("mm2")], [B("mm2")])
    dma(lambda e: e.dma_start(out=gmx_in.ap(), in_=mm2[0:1, :]), [B("mm2")], [B("gmx_in")], "gmx_st")
    fw.collective(lambda g: g.collective_compute("AllReduce", ALU.max, replica_groups=ALL8,
                                                 ins=[gmx_in.ap().opt()], outs=[gmx_out.ap().opt()]),
                  reads=[B("gmx_in")], writes=[B("gmx_out")], tag="gmx_ar")
    dma(lambda e: e.dma_start(out=mm2[0:1, :], in_=gmx_out.ap()),
        [B("mm2"), B("gmx_out")], [B("mm2")], "gmx_ld")
    vop(lambda v: v.tensor_tensor(scal[0:1, 20:21], mm2[0:1, 0:1], mm2[0:1, 1:2], ALU.add),
        [B("mm2")], [B("scal")], "q2_rng")
    vop(lambda v: v.tensor_scalar(scal[0:1, 10:11], scal[0:1, 20:21], 1.0 / 255.0, 1e-8,
                                  op0=ALU.mult, op1=ALU.max), [B("scal")], [B("scal")], "q2_s")
    vop(lambda v: v.reciprocal(scal[0:1, 11:12], scal[0:1, 10:11]), [B("scal")], [B("scal")], "q2_rs")
    vop(lambda v: v.tensor_tensor(scal[0:1, 18:19], mm2[0:1, 1:2], scal[0:1, 11:12], ALU.mult),
        [B("mm2"), B("scal")], [B("scal")], "q2_z0")
    sop(lambda s: s.activation(i32s[0:1, 0:1], scal[0:1, 18:19], AF.Copy, bias=RND),
        [B("scal"), B("constc")], [B("i32s")], "q2_zi")
    vop(lambda v: v.tensor_scalar(scal[0:1, 18:19], i32s[0:1, 0:1], 256.0 + 255.0, 256.0,
                                  op0=ALU.min, op1=ALU.max), [B("i32s")], [B("scal")], "q2_zc")
    vop(lambda v: v.tensor_scalar(scal[0:1, 12:13], scal[0:1, 18:19], -256.0 - 128.0, None,
                                  op0=ALU.add), [B("scal")], [B("scal")], "q2_zs")
    vop(lambda v: v.tensor_tensor(scal[0:1, 15:16], scal[0:1, 10:11], scal[0:1, 13:14], ALU.mult),
        [B("scal")], [B("scal")], "q2_sx")
    vop(lambda v: v.tensor_tensor(scal[0:1, 19:20], scal[0:1, 12:13], scal[0:1, 15:16], ALU.mult),
        [B("scal")], [B("scal")], "q2_zx")
    vop(lambda v: v.tensor_scalar(scal[0:1, 25:26], scal[0:1, 12:13], 1.0, scal[0:1, 23:24],
                                  op0=ALU.mult, op1=ALU.add),
        [B("scal")], [B("scal")], "q2_zpr")
    for (srcslot, col, extra) in ((11, 4, 0.0), (25, 5, 0.0), (15, 6, 0.0), (19, 7, 0.0)):
        top(lambda t, srcslot=srcslot: t.matmul(p7[:, 0:1], onesrf[:, :],
                                                scal[0:1, srcslot:srcslot + 1],
                                                start=True, stop=True),
            [B("scal"), B("onesrf"), B("p7")], [B("p7")], "q2_bc")
        if extra:
            vop(lambda v, col=col, extra=extra: v.tensor_scalar(
                    scalb[:, col:col + 1], p7[:, 0:1], extra, None, op0=ALU.add),
                [B("p7")], [B("scalb")], "q2_bca")
        else:
            sop(lambda s, col=col: s.copy(out=scalb[:, col:col + 1], in_=p7[:, 0:1]),
                [B("p7")], [B("scalb")], "q2_bcc")
    vop(lambda v: v.scalar_tensor_tensor(be2[:, :], cs2[:, :], scalb[:, 7:8], b2c_t[:, :],
                                         ALU.mult, ALU.subtract),
        [B("cs2"), B("scalb"), B("b2c")], [B("be2")], "be2")
    vop(lambda v: v.tensor_scalar(be2[:, :], be2[:, :], -1.0, None, op0=ALU.mult),
        [B("be2")], [B("be2")], "be2n")

    fw.phase_barrier()
    ffnA.close()
    ffnB = contextlib.ExitStack()
    gq_t = sb("gq_t", [128, NHC, 512], BF16, ffnB)
    gi322 = [sb(f"gi32{i}", [128, 512], I32, ffnB) for i in range(2)]
    e2_t = sb("e2_t", [128, NDC, 512], F32, ffnB)
    out_t = sb("out_t", [128, D], F32, ffnB)
    for w in range(FW):
        for hc in range(NHC):
            bi = hc % 2
            g_t, gi32 = g_t2[bi], gi322[bi]
            gb_, gib = B(f"g_t{bi}"), B(f"gi32{bi}")
            dma(lambda e, hc=hc, w=w, g_t=g_t: e.dma_start(
                    out=g_t[:, :], in_=g_b.ap()[bass.ts(hc, 128), bass.ts(w, 512)]),
                [gb_, B("g_bd")], [gb_], "f2_gld")
            vop(lambda v, g_t=g_t, gi32=gi32: v.tensor_scalar(
                    gi32[:, :], g_t[:, :], scalb[:, 4:5], scalb[:, 5:6], op0=ALU.mult, op1=ALU.add),
                [gb_, B("scalb")], [gib], "gq_i")
            vop(lambda v, g_t=g_t, gi32=gi32: v.tensor_scalar(
                    g_t[:, :], gi32[:, :], 256.0 + 127.0, 256.0 - 128.0, op0=ALU.min, op1=ALU.max),
                [gib], [gb_], "gq_c")
            vop(lambda v, hc=hc, g_t=g_t: v.tensor_scalar(gq_t[:, hc, :], g_t[:, :], -256.0, None,
                                                 op0=ALU.add),
                [gb_], [B("gq_t")], "gq_s")
        for dcol in range(NDC):
            pb, pbuf = (p0, B("p0")) if dcol % 2 == 0 else (p1, B("p1"))
            top(lambda t, dcol=dcol, pb=pb: [
                    t.matmul(pb[:, :], wq2T[:, hc, bass.ts(dcol, 128)], gq_t[:, hc, :],
                             start=(hc == 0), stop=(hc == NHC - 1)) for hc in range(NHC)][-1],
                [B("wq2T"), B("gq_t"), pbuf], [pbuf], "f2_mm")
            vop(lambda v, dcol=dcol, pb=pb: v.tensor_scalar(
                    e2_t[:, dcol, :], pb[:, :], scalb[:, 6:7], be2[:, dcol:dcol + 1],
                    op0=ALU.mult, op1=ALU.add),
                [pbuf, B("scalb"), B("be2")], [B("e2_t")], "f2_ev")
            vop(lambda v, dcol=dcol: v.tensor_scalar(
                    e2_t[:, dcol, :], e2_t[:, dcol, :], gbcol[:, dcol:dcol + 1], None, op0=ALU.mult),
                [B("e2_t"), B("gbcol")], [B("e2_t")], "f2_gb")
        for ssub in range(4):
            t_i = w * 4 + ssub
            for half in range(2):
                for dd in range(4):
                    top(lambda t, ssub=ssub, half=half, dd=dd: t.transpose(
                            p7[:, bass.ts(dd, 128)],
                            e2_t[:, half * 4 + dd, bass.ts(ssub, 128)], identf[:, :]),
                        [B("e2_t"), B("identf")], [B("p7")], "f2_T")
                dma(lambda e, t_i=t_i, half=half: e.dma_start(
                        out=xr_t[:, bass.ts(half, 512)],
                        in_=xres_b.ap()[bass.ts(t_i, 128), bass.ts(half, 512)]),
                    [B("xr_t"), B("xres_bd")], [B("xr_t")], "pc_xld")
                vop(lambda v, half=half: v.tensor_tensor(
                        out_t[:, bass.ts(half, 512)], xr_t[:, bass.ts(half, 512)], p7[:, :],
                        ALU.add),
                    [B("xr_t"), B("p7")], [B("out_t")], "f2_add")
            dma(lambda e, t_i=t_i: e.dma_start(out=out_own.ap()[bass.ts(t_i, 128), :],
                                               in_=out_t[:, :]),
                [B("out_t")], [B("out_d")], "out_st")

    fw.final_dma_barrier("sync")
    fw.finalize(ex)
    ffnB.close()
    ffn.close()
    ex.close()
    return nc


_NC_CACHE = None


def kernel(L0_attn_x, L0_attn_W_in, L0_attn_conv_w, L0_attn_conv_b, L0_attn_W_x,
           L0_attn_W_dt, L0_attn_b_dt, L0_attn_A_log, L0_attn_D, L0_attn_W_out,
           L0_ffn_W1, L0_ffn_b1, L0_ffn_W2, L0_ffn_b2, L0_c, L0_wmod):
    global _NC_CACHE
    if _NC_CACHE is None:
        _NC_CACHE = build_nc()
    nc = _NC_CACHE

    f = np.float32
    X = np.ascontiguousarray(np.asarray(L0_attn_x, dtype=f))
    W_in = np.asarray(L0_attn_W_in, f)
    conv_w = np.asarray(L0_attn_conv_w, f)
    conv_b = np.asarray(L0_attn_conv_b, f)
    W_x = np.asarray(L0_attn_W_x, f)
    W_dt = np.asarray(L0_attn_W_dt, f)
    b_dt = np.asarray(L0_attn_b_dt, f)
    A_log = np.asarray(L0_attn_A_log, f)
    Dv = np.asarray(L0_attn_D, f)
    W_out = np.asarray(L0_attn_W_out, f)
    W1 = np.asarray(L0_ffn_W1, f)
    b1 = np.asarray(L0_ffn_b1, f)
    W2 = np.asarray(L0_ffn_W2, f)
    b2 = np.asarray(L0_ffn_b2, f)
    Cc = np.asarray(L0_c, f)
    Wmod = np.asarray(L0_wmod, f)

    def cols(v):
        return np.ascontiguousarray(v.reshape(NCH, 128).T)

    w1t = np.ascontiguousarray(W1.T)
    b1c = np.ascontiguousarray(b1.reshape(NHC, 128).T)
    w2t = np.ascontiguousarray(W2.T)
    b2c = np.ascontiguousarray(b2.reshape(NDC, 128).T)
    ident = np.eye(128, dtype=f)
    ones_r = np.ones((1, 128), f)
    tri_c = np.ascontiguousarray(np.triu(np.ones((128, 128), f)))
    const_c = np.ascontiguousarray(
        np.tile(np.array([[1e-5, 1.0, RND, 0.6]], f), (128, 1)))

    in_maps = []
    for c in range(8):
        b_i, j = c // 2, c % 2
        ch = slice(j * DH, (j + 1) * DH)
        in_maps.append({
            "x_own": np.ascontiguousarray(X[b_i, j * SH:(j + 1) * SH, :]),
            "c_own": np.ascontiguousarray(Cc[b_i:b_i + 1, :]),
            "wmod": Wmod,
            "w_in_own": np.ascontiguousarray(
                np.concatenate([W_in[:, j * DH:(j + 1) * DH],
                                W_in[:, DI + j * DH:DI + (j + 1) * DH]], axis=1)),
            "conv_w_c": np.ascontiguousarray(
                conv_w[ch].reshape(NCH, 128, 4).transpose(1, 0, 2)),
            "conv_b_c": cols(conv_b[ch]),
            "w_x_own": np.ascontiguousarray(W_x[ch]),
            "w_dt_own": np.ascontiguousarray(W_dt[:, ch]),
            "b_dt_c": cols(b_dt[ch]),
            "b_dt_row": np.ascontiguousarray(b_dt[ch][None, :]),
            "a_log_c": np.ascontiguousarray(
                A_log[ch].reshape(NCH, 128, KS).transpose(1, 0, 2)),
            "d_c": cols(Dv[ch]),
            "w_out_own": np.ascontiguousarray(W_out[ch]),
            "w1t": w1t,
            "b1_c": b1c,
            "w2t": w2t,
            "b2_c": b2c,
            "ident_in": ident,
            "ones_r": ones_r,
            "tri_in": tri_c,
            "const_in": const_c,
        })

    res = run_bass_kernel_spmd(nc, in_maps, core_ids=list(range(8)))
    out = np.empty((4, S, D), f)
    for c in range(8):
        b_i, j = c // 2, c % 2
        out[b_i, j * SH:(j + 1) * SH, :] = np.asarray(res.results[c]["out_own"], f)
    return out
